# version 17
# baseline (speedup 1.0000x reference)
import os
import numpy as np
import ml_dtypes
import concourse.bass as bass
import concourse.mybir as mybir
from concourse.bass_utils import run_bass_kernel_spmd

F32 = mybir.dt.float32
BF16 = mybir.dt.bfloat16
I32 = mybir.dt.int32
AF = mybir.ActivationFunctionType
ALU = mybir.AluOpType
AX = mybir.AxisListType

ENGS = ("sync", "scalar", "vector", "tensor", "gpsimd")

T = 2048
D = 1024
NT = T // 128
NC = D // 128
NU = 32
NST = 16
EPS_LN = 1e-5
EPS_RMS = 1e-6
EPS_ACT = 1e-5
RND = 12582912.0
PHASES = int(os.environ.get("MK_PHASES", "5"))
SUB = int(os.environ.get("MK_SUB", "9"))


class KB:
    def __init__(self):
        self.streams = {e: [] for e in ENGS}
        self.cnt = {}

    AUTO = {"vector": "v", "scalar": "s", "gpsimd": "g"}

    def op(self, eng, fn, waits=(), inc=None, amt=1):
        waits = tuple((s, v) for (s, v) in
                      ((t for t in waits if t is not None))
                      if v is not None and v > 0)
        if eng in self.AUTO:
            ssem = self.AUTO[eng]
            sval = self.cnt.get(ssem, 0)
            if sval > 0:
                waits = waits + ((ssem, sval),)
            if inc is None:
                inc = ssem
        for s, _ in waits:
            self.cnt.setdefault(s, 0)
        if inc is not None:
            self.cnt.setdefault(inc, 0)
            self.cnt[inc] += amt
        self.streams[eng].append((fn, waits, inc, amt))
        return (inc, self.cnt[inc]) if inc else None

    def dma(self, q, out, in_, waits=(), sem=None, slow=False):
        semn = sem if sem is not None else {"sync": "dS", "scalar": "dC",
                                            "gpsimd": "dG", "vector": "dV"}[q]
        return self.op(q, lambda e, o=out, i=in_, sl=slow: e.dma_start(
            o, i, allow_slow_non_contiguous=sl) if sl else e.dma_start(o, i),
            waits=waits, inc=semn, amt=16)

    def wait(self, eng, *toks):
        toks = tuple((s, v) for (s, v) in
                     (t for t in toks if t is not None)
                     if s is not None and v is not None and v > 0)
        for s, _ in toks:
            self.cnt.setdefault(s, 0)
        if toks:
            self.streams[eng].append((None, toks, None, 0))

    def tok(self, semname):
        return (semname, self.cnt.get(semname, 0))

    def barrier(self):
        toks = tuple((s, v) for s, v in self.cnt.items() if v > 0)
        for e in ENGS:
            self.wait(e, *toks)

    def emit(self, nc):
        import contextlib
        with contextlib.ExitStack() as st:
            sems = {n: st.enter_context(nc.semaphore(n))
                    for n in sorted(self.cnt)}
            blk = st.enter_context(nc.Block())

            def mk(items):
                def f(e):
                    for fn, waits, inc, amt in items:
                        for s, v in waits:
                            e.wait_ge(sems[s], v)
                        if fn is not None:
                            ins = fn(e)
                            if inc is not None:
                                ins.then_inc(sems[inc], amt)
                return f
            for eng in ENGS:
                if self.streams[eng]:
                    getattr(blk, eng)(mk(self.streams[eng]))


def _pm(a):
    a = np.asarray(a)
    if a.ndim == 1:
        a = a[:, None]
    k = a.shape[1]
    return np.ascontiguousarray(
        a.reshape(NC, 128, k).transpose(1, 0, 2).reshape(128, NC * k))


def _wpack(w):
    w = np.asarray(w)
    return np.ascontiguousarray(w.reshape(NC, 128, w.shape[1]).transpose(1, 0, 2))


def _upack(w):
    w = np.asarray(w).reshape(NC, 128, NU, 128)
    return np.ascontiguousarray(w.transpose(2, 1, 0, 3))


def build_program(wscale_val):
    nc = bass.Bass("TRN2", target_bir_lowering=False, debug=False)
    kb = KB()

    def din(name, shape, dt=F32):
        return nc.dram_tensor(name, shape, dt, kind="ExternalInput").ap()

    x_in = din("x", [128, NT, D])
    w_inT = din("w_inT", [128, NC, D])
    w_outT = din("w_outT", [128, NC, D])
    ffn0_rhs = din("ffn0_rhs", [128, NC, D])
    wk_u = din("wk_u", [NU, 128, NC, 128])
    wv_u = din("wv_u", [NU, 128, NC, 128])
    wr_u = din("wr_u", [NU, 128, NC, 128])
    wo_u = din("wo_u", [NU, 128, D])
    cwk_u = din("cwk_u", [NU, 128, NC, 128])
    cwv_u = din("cwv_u", [NU, 128, D])
    cwrT = din("cwrT", [128, NC, D])
    wffT = din("wffT", [128, NC, D], I32)
    prm = din("prm", [128, 544])
    ln0g_v = din("ln0g_v", [D])
    ln0b_v = din("ln0b_v", [D])
    bias_v = din("bias_v", [D])
    ident_bd = din("ident_b", [128, 128], BF16)
    ident_fd = din("ident_f", [128, 128], F32)

    out_d = nc.dram_tensor("out", [128, NT, D], F32, kind="ExternalOutput").ap()
    xpark = nc.dram_tensor("xpark", [128, NT, D], F32, kind="Internal").ap()
    wkb_d = nc.dram_tensor("wkb_d", [NU, 128, D], BF16, kind="Internal").ap()
    wvb_d = nc.dram_tensor("wvb_d", [NU, 128, D], BF16, kind="Internal").ap()
    wrb_d = nc.dram_tensor("wrb_d", [NU, 128, D], BF16, kind="Internal").ap()
    wob_d = nc.dram_tensor("wob_d", [NU, 128, D], BF16, kind="Internal").ap()
    cwkb_d = nc.dram_tensor("cwkb_d", [NU, 128, D], BF16, kind="Internal").ap()
    cwvb_d = nc.dram_tensor("cwvb_d", [NU, 128, D], BF16, kind="Internal").ap()
    rs_rt = nc.dram_tensor("rs_rt", [2 * D], F32, kind="Internal").ap()

    WSCALE = float(wscale_val)

    PRM = {}
    off = 0
    for name, w in [("A_log", 128), ("B_ssm", 128), ("C_ssm", 128),
                    ("dt_log", 8), ("D_ssm", 8), ("decay", 32), ("first", 32),
                    ("mix_k", 8), ("mix_v", 8), ("mix_r", 8),
                    ("cmix_k", 8), ("cmix_r", 8), ("pre_g", 8), ("pre_b", 8),
                    ("post_g", 8), ("post_b", 8)]:
        PRM[name] = off
        off += w
    assert off <= 544

    SB_BASE = 0x4140

    def sb(name, offset, shape, dtype):
        return nc.alloc_sbuf_tensor_at(name, shape, dtype,
                                       offset=SB_BASE + offset)

    XT = sb("XT", 0x00000, [128, NT, D], F32)
    XK = sb("XK", 0x00000, [128, NC, 1024], BF16)
    XV = sb("XV", 0x04000, [128, NC, 1024], BF16)
    XR = sb("XR", 0x08000, [128, NC, 1024], BF16)
    RW = sb("RW", 0x0C000, [128, 8, 1024], BF16)
    XK2 = sb("XK2", 0x00000, [128, NC, T], BF16)
    XR2 = sb("XR2", 0x08000, [128, NC, 1024], BF16)
    CWR = sb("CWR", 0x0C000, [128, NC, D], BF16)
    GSL = sb("GSL", 0x08000, [128, 8, T], BF16)

    YT = sb("YT", 0x10000, [128, NT, D], F32)
    ZFM = sb("ZFM", 0x10000, [128, NC, T], BF16)
    YBC = sb("YBC", 0x10000, [128, NC, T], BF16)
    QTM = sb("QTM", 0x10000, [128, NT, D], BF16)
    XBF = sb("XBF", 0x18000, [128, NC, T], BF16)
    QFM = sb("QFM", 0x18000, [128, NC, T], BF16)

    WIN = sb("WIN", 0x20000, [128, NC, D], BF16)
    WOUT = sb("WOUT", 0x20000, [128, NC, D], BF16)
    QRHS = sb("QRHS", 0x20000, [128, NC, D], BF16)
    WFF = sb("WFF", 0x20000, [128, NC, D], BF16)
    XNH = sb("XNH", 0x20000, [128, NC, 1028], BF16)
    WOS = sb("WOS", 0x20000, [128, 8, D], BF16)
    YAC = sb("YAC", 0x24000, [128, T], F32)
    RSTG = sb("RSTG", 0x24000, [128, 2, D], F32)
    ISTG = sb("ISTG", 0x24000, [128, D], I32)
    HB = sb("HB", 0x26000, [128, T], F32)
    HBB = sb("HBB", 0x30000, [128, T], F32)
    HBC = sb("HBC", 0x2B960, [128, T], F32)
    WSTG3 = sb("WSTG3", 0x26000, [128, D], F32)
    SC2 = sb("SC2", 0x26000, [128, D], F32)
    EB3 = sb("EB3", 0x27000, [128, D], F32)
    EB4 = sb("EB4", 0x31000, [128, D], F32)
    B1 = sb("B1", 0x24100, [128, 1024], F32)
    B3 = sb("B3", 0x25100, [128, 1024], F32)
    B2 = sb("B2", 0x26100, [128, 1024], F32)
    B4 = sb("B4", 0x27100, [128, 1024], F32)
    B5 = sb("B5", 0x32000, [128, 1024], F32)
    SGA = sb("SGA", 0x30000, [128, 1024], F32)
    SGB = sb("SGB", 0x31000, [128, 1024], F32)
    RST1 = sb("RST1", 0x32000, [128, D], F32)
    WPS0 = sb("WPS0", 0x2C960, [128, NC, 128], F32)
    WPS1 = sb("WPS1", 0x30000, [128, NC, 128], F32)
    WPS2 = sb("WPS2", 0x31000, [128, NC, 128], F32)
    XIN2 = sb("XIN2", 0x20000, [128, 2, T], F32)

    ST = sb("ST", 0x28400, [128, 256], F32)
    PRMT = sb("PRMT", 0x28800, [128, 544], F32)
    DRV = sb("DRV", 0x29100, [128, 344], F32)
    IDB = sb("IDB", 0x29660, [128, 128], BF16)
    IDF = sb("IDF", 0x29760, [128, 128], F32)
    BCA = sb("BCA", 0x29960, [128, D], F32)
    BCB = sb("BCB", 0x2A960, [128, D], F32)
    LNS = sb("LNS", 0x2B960, [128, D], F32)
    TMB = sb("TMB", 0x2C160, [128, 1024], BF16)
    SC1 = sb("SC1", 0x2C960, [128, D], F32)
    SC1W = sb("SC1W", 0x2C960, [128, NC, 128], F32)
    WKB = sb("WKB", 0x2D960, [128, NC, 128], BF16)
    WVB = sb("WVB", 0x2E160, [128, NC, 128], BF16)
    WRB = sb("WRB", 0x2E960, [128, NC, 128], BF16)
    XST = sb("XST", 0x2D960, [128, 2, D], F32)
    BND = sb("BND", 0x2F960, [128, 8], BF16)
    CV1 = sb("CV1", 0x30000, [128, 1024], F32)
    CV2 = sb("CV2", 0x31000, [128, 1024], F32)
    assert SB_BASE + 0x32000 <= 229344

    DV = {}
    doff = 0
    for name, w in [("a_bar", 128), ("cb", 128), ("decay", 32), ("first", 32),
                    ("dt", 8), ("misc", 16)]:
        DV[name] = doff
        doff += w
    assert doff <= 344

    PS0 = nc.alloc_psum_tensor("PS0", [128, 1024], F32)
    PS1 = nc.alloc_psum_tensor("PS1", [128, 1024], F32)
    PS2 = nc.alloc_psum_tensor("PS2", [128, 1024], F32)
    PS3 = nc.alloc_psum_tensor("PS3", [128, 1024], F32)
    PSF = [PS0, PS1, PS2]

    C_EPSLN, C_EPSRMS, C_EPSACT, C_60, C_KEPS = 240, 241, 242, 243, 244

    def stc(i):
        return ST[:, i:i + 1]

    def pcol(name, j=0):
        o = PRM[name] + j
        return PRMT[:, o:o + 1]

    def dvcol(name, j=0):
        o = DV[name] + j
        return DRV[:, o:o + 1]

    def ln_stats(src_ap, g, waits_v=(), waits_s=(), eps_col=C_EPSLN,
                 base=136):
        b = base + 6 * (g % 8)
        s_c, q_c, m_c, n_c, r_c = (stc(b), stc(b + 1), stc(b + 2),
                                   stc(b + 3), stc(b + 4))
        kb.op("vector", lambda e: e.reduce_sum(s_c, src_ap, axis=AX.X),
              waits=waits_v)
        kb.op("vector", lambda e: e.tensor_scalar_mul(m_c, s_c, 1.0 / D))
        tv = kb.op("vector", lambda e: e.tensor_mul(n_c, m_c, m_c), inc="v")
        ts = kb.op("scalar", lambda e: e.activation(LNS[:], src_ap, AF.Square,
                                                    accum_out=q_c),
                   waits=waits_s, inc="s")
        tv2 = kb.op("vector", lambda e: e.scalar_tensor_tensor(
            r_c, q_c, 1.0 / D, n_c, ALU.mult, ALU.subtract),
            waits=(ts, tv), inc="v")
        tsq = kb.op("scalar", lambda e: e.activation(
            r_c, r_c, AF.Ln, bias=stc(eps_col)), waits=(tv2,), inc="s")
        tr = kb.op("scalar", lambda e: e.activation(
            r_c, r_c, AF.Exp, scale=-0.5), inc="s")
        return m_c, r_c, tr

    pst_state = {"half": 0, "tok": [None, None]}

    def transpose_tile(src, dst_sl, wait_src):
        h = pst_state["half"]
        base = h * 512
        last = None
        for j in range(NC):
            w = []
            if j == 0:
                w = [wait_src, pst_state["tok"][h]]
            last = kb.op("tensor", lambda e, j=j, base=base: e.matmul(
                PS3[:, base + j * 64: base + (j + 1) * 64].bitcast(BF16),
                src[:, j * 128:(j + 1) * 128], IDB[:],
                start=True, stop=True, is_transpose=True),
                waits=w, inc="t" if j == NC - 1 else None)
        ev = kb.op("scalar", lambda e, base=base: e.copy(
            dst_sl, PS3[:, base:base + 512].bitcast(BF16).rearrange(
                "p (c t) -> p c t", c=NC)), waits=(last,), inc="s")
        pst_state["tok"][h] = ev
        pst_state["half"] = 1 - h
        return last, ev

    t_x = kb.dma("sync", XT[:], x_in, sem="dX")
    t_prm = kb.dma("scalar", PRMT[:], prm, sem="dP")
    t_idb = kb.dma("scalar", IDB[:], ident_bd, sem="dP")
    t_idf = kb.dma("scalar", IDF[:], ident_fd, sem="dP")
    t_prm = t_idb = t_idf = kb.tok("dP")
    for col, val in [(C_EPSLN, EPS_LN), (C_EPSRMS, EPS_RMS),
                     (C_EPSACT, EPS_ACT), (C_60, 60.0), (C_KEPS, 1e-8)]:
        kb.op("vector", lambda e, c=col, v=val: e.memset(stc(c), v))
    t_cc = kb.op("vector", lambda e: e.memset(ST[:, 64:128], 0.0), inc="v")
    nc.const_aps.register(F32, 1e-8, stc(C_KEPS))

    prev = t_prm
    t_wincast = None
    for c in range(NC):
        td = kb.dma("sync", SC1[:], w_inT[:, c, :], waits=(prev,), sem="dW")
        t_wincast = kb.op("scalar", lambda e, c=c: e.copy(WIN[:, c, :], SC1[:]),
                          waits=(td,), inc="s")
        prev = t_wincast

    AL = PRMT[:, PRM["A_log"]:PRM["A_log"] + 128]
    BS = PRMT[:, PRM["B_ssm"]:PRM["B_ssm"] + 128]
    CS = PRMT[:, PRM["C_ssm"]:PRM["C_ssm"] + 128]
    ABAR = DRV[:, DV["a_bar"]:DV["a_bar"] + 128]
    CBV = DRV[:, DV["cb"]:DV["cb"] + 128]
    TDC = DRV[:, DV["decay"]:DV["decay"] + 32]
    TFC = DRV[:, DV["first"]:DV["first"] + 32]
    DTC = DRV[:, DV["dt"]:DV["dt"] + 8]
    kb.wait("scalar", t_prm)
    texpa = kb.op("scalar", lambda e: e.activation(ABAR, AL, AF.Exp), inc="s")
    tdt = kb.op("scalar", lambda e: e.activation(
        DTC, PRMT[:, PRM["dt_log"]:PRM["dt_log"] + 8], AF.Exp), inc="s")
    ts0 = kb.op("vector", lambda e: e.reciprocal(CBV, ABAR), waits=(texpa,),
                inc="v")
    tv0 = None
    for c in range(NC):
        tv0 = kb.op("vector", lambda e, c=c: e.tensor_scalar(
            ABAR[:, c * NST:(c + 1) * NST], ABAR[:, c * NST:(c + 1) * NST],
            DRV[:, DV["dt"] + c:DV["dt"] + c + 1], -1.0, ALU.mult, ALU.mult),
            waits=(ts0, tdt) if c == 0 else (), inc="v")
    ts1 = kb.op("scalar", lambda e: e.activation(ABAR, ABAR, AF.Exp),
                waits=(tv0,), inc="s")
    tv1 = kb.op("vector", lambda e: e.scalar_tensor_tensor(
        CBV, ABAR, 1.0, CBV, ALU.subtract, ALU.mult), waits=(ts1,), inc="v")
    tv2 = kb.op("vector", lambda e: e.tensor_mul(CBV, CBV, BS), inc="v")
    t_cb = kb.op("vector", lambda e: e.scalar_tensor_tensor(
        CBV, CBV, -1.0, CS, ALU.mult, ALU.mult), inc="v")
    ts2 = kb.op("scalar", lambda e: e.activation(
        TDC, PRMT[:, PRM["decay"]:PRM["decay"] + 32], AF.Exp), inc="s")
    ts3 = kb.op("scalar", lambda e: e.activation(TDC, TDC, AF.Exp, scale=-1.0),
                waits=(ts2,), inc="s")
    t_first = kb.op("scalar", lambda e: e.activation(
        TFC, PRMT[:, PRM["first"]:PRM["first"] + 32], AF.Exp), inc="s")

    kb.barrier()

    prev_pe = None
    for tc in range(NT if SUB >= 1 else 0):
        g = tc % 8
        m_c, r_c, tr = ln_stats(XT[:, tc, :], g, waits_v=(t_x,) if tc == 0 else (),
                                waits_s=(t_x,) if tc == 0 else ())
        wn = [tr]
        if prev_pe:
            wn.append(prev_pe)
        norm_tok = kb.op("vector", lambda e, tc=tc, m=m_c, r=r_c: e.tensor_scalar(
            TMB[:], XT[:, tc, :], m, r, ALU.subtract, ALU.mult),
            waits=tuple(wn), inc="v")
        prev_pe, _ = transpose_tile(
            TMB[:], ZFM[:, :, tc * 128:(tc + 1) * 128], norm_tok)
    t_zdone = kb.tok("s")

    t_evac = {}
    t_mm_last = None
    for ec in range(NC if SUB >= 2 else 0):
        w = [kb.tok("s")] if ec == 0 else [t_evac[ec - 1][1]]
        for d in range(NC):
            for s4 in range(4):
                ps = PS0 if s4 < 2 else PS1
                col = (s4 % 2) * 512
                t_mm_last = kb.op("tensor", lambda e, d=d, s4=s4, ps=ps, col=col,
                                  ec=ec: e.matmul(
                    ps[:, col:col + 512], WIN[:, d, ec * 128:(ec + 1) * 128],
                    ZFM[:, d, s4 * 512:(s4 + 1) * 512],
                    start=(d == 0), stop=(d == NC - 1)),
                    waits=tuple(w) if (d == 0 and s4 == 0) else (),
                    inc="t" if (d == NC - 1 and s4 == 3) else None)
        e1 = kb.op("scalar", lambda e, ec=ec: e.copy(XBF[:, ec, 0:1024], PS0[:]),
                   waits=(t_mm_last,), inc="s")
        e2 = kb.op("scalar", lambda e, ec=ec: e.copy(XBF[:, ec, 1024:2048], PS1[:]),
                   inc="s")
        t_evac[ec] = (e1, e2)
    t_win_mm_done = t_mm_last


    precast = {}
    wprep_items = []
    if PHASES >= 3:
        stg_in = [WPS0, WPS1, WPS2]
        stg_bf = [WKB, WVB, WRB]
        srcs_all = [("wk", wk_u, wkb_d, True), ("wv", wv_u, wvb_d, True),
                    ("wr", wr_u, wrb_d, True), ("wo", wo_u, wob_d, False),
                    ("cwk", cwk_u, cwkb_d, True), ("cwv", cwv_u, cwvb_d, False)]
        for kind, dsrc, ddst, is_chunk in srcs_all:
            precast[kind] = {}
            for c in range(NU):
                wprep_items.append((kind, c, dsrc, ddst))

    wprep_state = {"i": 0, "cast_tok": [t_wincast, None, None],
                   "out_tok": [None, None, None]}

    def emit_wprep(nmax):
        for _ in range(nmax):
            i = wprep_state["i"]
            if i >= len(wprep_items):
                return
            wprep_state["i"] = i + 1
            kind, c, dsrc, ddst = wprep_items[i]
            s3 = i % 3
            sin = [WPS0, WPS1, WPS2][s3]
            sbf = [WKB, WVB, WRB][s3]
            w_in = [wprep_state["cast_tok"][s3]]
            tdi = kb.dma("sync", sin[:], dsrc[c].rearrange(
                "p (c k) -> p c k", c=NC) if dsrc[c].shape == (128, D)
                else dsrc[c], waits=tuple(t for t in w_in if t),
                sem="dWp%d" % s3)
            wcast = [tdi]
            if wprep_state["out_tok"][s3]:
                wcast.append(wprep_state["out_tok"][s3])
            tc_ = kb.op("scalar", lambda e, sin=sin, sbf=sbf: e.copy(
                sbf[:], sin[:]), waits=tuple(wcast), inc="s")
            wprep_state["cast_tok"][s3] = tc_
            tdo = kb.dma("scalar", ddst[c].rearrange("p (c k) -> p c k", c=NC),
                         sbf[:], waits=(tc_,), sem="dWq%d" % s3)
            wprep_state["out_tok"][s3] = tdo
            precast[kind][c] = tdo

    if PHASES >= 2:
        tred = None
        slot_use_r = None
        for c in range(NC):
            wdma = [slot_use_r] if slot_use_r else []
            tdma = kb.dma("sync", RST1[:], ffn0_rhs[:, c, :],
                          waits=tuple(wdma), sem="dR0")
            tt = None
            last_red = [None, None]
            for j in range(NC):
                h2 = j % 2
                po = h2 * 512
                wt = [tdma, t_idf]
                if last_red[h2]:
                    wt.append(last_red[h2])
                elif tred:
                    wt.append(tred)
                tt = kb.op("tensor", lambda e, c=c, j=j, po=po: e.matmul(
                    PS2[:, po:po + 128], RST1[:, j * 128:(j + 1) * 128],
                    IDF[:], start=True, stop=True, is_transpose=True),
                    waits=tuple(wt), inc="t")
                kb.op("vector", lambda e, po=po, h2=h2: e.reduce_max(
                    stc(32 + h2), PS2[:, po:po + 128], axis=AX.X,
                    apply_absolute_value=True), waits=(tt,))
                if c == 0:
                    tred = kb.op("vector", lambda e, j=j, h2=h2: e.tensor_copy(
                        stc(40 + j), stc(32 + h2)), inc="v")
                else:
                    tred = kb.op("vector", lambda e, j=j, h2=h2: e.tensor_max(
                        stc(40 + j), stc(40 + j), stc(32 + h2)), inc="v")
                last_red[h2] = tred
            slot_use_r = tt
        t_absmax_done = tred
        tcl = kb.op("vector", lambda e: e.tensor_scalar_max(
            ST[:, 40:48], ST[:, 40:48], 1e-12), waits=(tred,), inc="v")
        kb.op("scalar", lambda e: e.activation(
            ST[:, 48:56], ST[:, 40:48], AF.Ln, scale=1.0 / 127.0),
            waits=(tcl,))
        trc = kb.op("scalar", lambda e: e.activation(
            ST[:, 48:56], ST[:, 48:56], AF.Exp, scale=-1.0), inc="s")
        tds = kb.op("vector", lambda e: e.tensor_scalar_mul(
            ST[:, 56:64], ST[:, 40:48], 1.0 / 127.0), waits=(tcl,), inc="v")
        tp1 = kb.dma("sync", rs_rt[0:D].rearrange("(c p) -> p c", p=128),
                     ST[:, 48:56], waits=(trc,), slow=True, sem="dB")
        tp2 = kb.dma("sync", rs_rt[D:2 * D].rearrange("(c p) -> p c", p=128),
                     ST[:, 56:64], waits=(tds, tp1), slow=True, sem="dB")
        tb1 = kb.dma("sync", BCB[:],
                     rs_rt[0:D].unsqueeze(0).broadcast_to((128, D)),
                     waits=(tp2,), sem="dB")
        tb2 = kb.dma("sync", BCA[:],
                     rs_rt[D:2 * D].unsqueeze(0).broadcast_to((128, D)),
                     waits=(tb1,), sem="dB")

    t_cast_y = {}
    scan_last = {}
    for c in range(NC if SUB >= 3 else 0):
        wcv = [t_evac[c][1]]
        if c >= 2:
            wcv.append(scan_last[c - 2])
        t_cvt = kb.op("scalar", lambda e, c=c: e.copy(
            XIN2[:, c % 2, :], XBF[:, c, :]), waits=tuple(wcv), inc="s")
        wv = [t_cvt, t_cb]
        if c > 0:
            wv.append(t_cast_y[c - 1])
        tini = kb.op("vector", lambda e, c=c: e.tensor_scalar_mul(
            YAC[:], XIN2[:, c % 2, :], pcol("D_ssm", c)), waits=tuple(wv),
            inc="v")
        last_add = None
        for n in range(NST):
            acol = DRV[:, DV["a_bar"] + c * NST + n:DV["a_bar"] + c * NST + n + 1]
            ccol = DRV[:, DV["cb"] + c * NST + n:DV["cb"] + c * NST + n + 1]
            kb.op("vector", lambda e, c=c, a=acol:
                  e.tensor_tensor_scan(
                      HB[:, 0:1024], a.broadcast_to((128, 1024)),
                      XIN2[:, c % 2, 0:1024], 0.0, ALU.mult, ALU.add))
            tsc = kb.op("vector", lambda e, c=c, a=acol:
                        e.tensor_tensor_scan(
                            HB[:, 1024:2048], a.broadcast_to((128, 1024)),
                            XIN2[:, c % 2, 1024:2048], HB[:, 1023:1024],
                            ALU.mult, ALU.add), inc="v")
            scan_last[c] = tsc
            last_add = kb.op("vector", lambda e, cc=ccol: e.scalar_tensor_tensor(
                YAC[:], HB[:], cc, YAC[:], ALU.mult, ALU.add), inc="v")
        wv2 = [last_add, t_win_mm_done]
        t_cast_y[c] = kb.op("scalar", lambda e, c=c: e.copy(YBC[:, c, :], YAC[:]),
                            waits=tuple(wv2), inc="s")
        emit_wprep(26)

    emit_wprep(10 ** 6)
    t_wprep_done = kb.tok("s")

    kb.wait("sync", kb.tok("g"))
    prev = kb.tok("v")
    t_woutcast = None
    for c in range(NC if SUB >= 4 else 0):
        td = kb.dma("sync", WSTG3[:], w_outT[:, c, :], waits=(prev,), sem="dW")
        t_woutcast = kb.op("scalar", lambda e, c=c: e.copy(WOUT[:, c, :],
                                                           WSTG3[:]),
                           waits=(td,), inc="s")
        prev = t_woutcast

    ps_free = [None, None, None]
    add_toks = [None, None, None, None]
    add_p1 = {}
    EBS = [SC1, SC2, EB3, EB4]
    if PHASES >= 2:
        ps_free[2] = t_absmax_done
    for tc in range(NT if SUB >= 4 else 0):
        ps = PSF[tc % 3]
        w = [t_cast_y[NC - 1], t_woutcast]
        if ps_free[tc % 3]:
            w.append(ps_free[tc % 3])
        tl = None
        for ecc in range(NC):
            for os in range(2):
                tl = kb.op("tensor", lambda e, ecc=ecc, os=os, ps=ps, tc=tc:
                           e.matmul(ps[:, os * 512:(os + 1) * 512],
                                    YBC[:, ecc, tc * 128:(tc + 1) * 128],
                                    WOUT[:, ecc, os * 512:(os + 1) * 512],
                                    start=(ecc == 0), stop=(ecc == NC - 1)),
                           waits=tuple(w) if (ecc == 0 and os == 0) else (),
                           inc="t" if (ecc == NC - 1 and os == 1) else None)
        eb = EBS[tc % 4]
        wev = [tl]
        if add_toks[tc % 4]:
            wev.append(add_toks[tc % 4])
        tev = kb.op("scalar", lambda e, ps=ps, eb=eb: e.copy(eb[:], ps[:]),
                    waits=tuple(wev), inc="s")
        ps_free[tc % 3] = tev
        g = tc % 8
        m_c, r_c, tr = ln_stats(eb[:], g, waits_v=(tev,), waits_s=(tev,))
        tn = kb.op("vector", lambda e, m=m_c, r=r_c, eb=eb: e.tensor_scalar(
            eb[:], eb[:], m, r, ALU.subtract, ALU.mult), waits=(tr,),
            inc="v")
        add_toks[tc % 4] = kb.op("gpsimd", lambda e, tc=tc, eb=eb: e.tensor_add(
            XT[:, tc, :], XT[:, tc, :], eb[:]), waits=(tn,), inc="g")
        add_p1[tc] = add_toks[tc % 4]

    t_p1_mm_done = kb.tok("t")
    if PHASES >= 2:

        tq_tile = {}
        qrot = list(add_toks)
        prev_round = None
        for tc in range(NT):
            g = tc % 8
            qb = EBS[tc % 4]
            wv_t = [add_p1[tc]] if tc in add_p1 else []
            m_c, r_c, tr = ln_stats(
                XT[:, tc, :], g, waits_v=tuple(wv_t),
                waits_s=tuple(wv_t) + ((prev_round,) if prev_round else ()),
                base=184)
            wn = [tr]
            if qrot[tc % 4]:
                wn.append(qrot[tc % 4])
            tn = kb.op("vector", lambda e, tc=tc, m=m_c, r=r_c, qb=qb:
                       e.tensor_scalar(qb[:], XT[:, tc, :], m, r,
                                       ALU.subtract, ALU.mult),
                       waits=tuple(wn), inc="v")
            tam = kb.op("vector", lambda e, qb=qb: e.reduce_max(
                stc(128), qb[:], axis=AX.X, apply_absolute_value=True),
                inc="v")
            tcl2 = kb.op("vector", lambda e: e.tensor_scalar_max(
                stc(128), stc(128), 1e-12), inc="v")
            kb.op("scalar", lambda e: e.activation(
                stc(129), stc(128), AF.Ln, scale=1.0 / 127.0), waits=(tcl2,))
            tlsr = kb.op("scalar", lambda e: e.activation(
                stc(129), stc(129), AF.Exp, scale=-1.0), inc="s")
            kb.op("vector", lambda e, tc=tc: e.tensor_scalar_mul(
                stc(24 + tc), stc(128), 1.0 / 127.0), inc="v")
            wq1 = [tlsr]
            if prev_round:
                wq1.append(prev_round)
            tq1 = kb.op("scalar", lambda e, qb=qb: e.activation(
                LNS[:], qb[:], AF.Copy, scale=stc(129)), waits=tuple(wq1),
                inc="s")
            qrot[tc % 4] = tq1
            tq2 = kb.op("vector", lambda e, tc=tc: e.tensor_scalar(
                QTM[:, tc, :], LNS[:], RND, -RND, ALU.add, ALU.add),
                waits=(tq1,), inc="v")
            prev_round = tq2
            _, tq_tile[tc] = transpose_tile(
                QTM[:, tc, :], QFM[:, :, tc * 128:(tc + 1) * 128], tq2)

        tqr = None
        slot_use_r2 = kb.tok("t")
        for c in range(NC):
            wdma = [tb2, slot_use_r2]
            if tqr:
                wdma.append(tqr)
            tdma = kb.dma("sync", RST1[:], ffn0_rhs[:, c, :],
                          waits=tuple(wdma), sem="dR0")
            wq = [tdma, tb1]
            if c == 0:
                wq.append(t_wprep_done)
            t1 = kb.op("vector", lambda e, c=c: e.tensor_mul(
                HBB[:, 0:1024], RST1[:], BCB[:]), waits=tuple(wq), inc="v")
            tqr = kb.op("vector", lambda e, c=c: e.tensor_scalar(
                QRHS[:, c, :], HBB[:, 0:1024], RND, -RND, ALU.add, ALU.add),
                waits=(t_p1_mm_done,) if c == 0 else (), inc="v")
        t_qrhs = tqr

        add_toks = list(qrot)
        for tc in range(NT):
            ps = PSF[tc % 3]
            w = [tq_tile[tc], t_qrhs]
            if ps_free[tc % 3]:
                w.append(ps_free[tc % 3])
            tl = None
            for d in range(NC):
                for os in range(2):
                    tl = kb.op("tensor", lambda e, d=d, os=os, ps=ps, tc=tc:
                               e.matmul(ps[:, os * 512:(os + 1) * 512],
                                        QFM[:, d, tc * 128:(tc + 1) * 128],
                                        QRHS[:, d, os * 512:(os + 1) * 512],
                                        start=(d == 0), stop=(d == NC - 1)),
                               waits=tuple(w) if (d == 0 and os == 0) else (),
                               inc="t" if (d == NC - 1 and os == 1) else None)
            wn = [tl, tb2]
            if add_toks[tc % 4]:
                wn.append(add_toks[tc % 4])
            eb = EBS[tc % 4]
            ta = kb.op("scalar", lambda e, ps=ps, tc=tc, eb=eb: e.activation(
                eb[:], ps[:], AF.Copy, scale=stc(24 + tc)), waits=tuple(wn),
                inc="s")
            ps_free[tc % 3] = ta
            tb = kb.op("gpsimd", lambda e, eb=eb: e.tensor_mul(
                eb[:], eb[:], BCA[:]), waits=(ta,), inc="g")
            g = tc % 8
            m_c, r_c, tr = ln_stats(eb[:], g, waits_v=(tb,), waits_s=(tb,))
            tn = kb.op("vector", lambda e, m=m_c, r=r_c, eb=eb: e.tensor_scalar(
                eb[:], eb[:], m, r, ALU.subtract, ALU.mult), waits=(tr,),
                inc="v")
            add_toks[tc % 4] = kb.op("gpsimd", lambda e, tc=tc, eb=eb:
                                     e.tensor_add(XT[:, tc, :], XT[:, tc, :],
                                                  eb[:]), waits=(tn,), inc="g")

        kb.barrier()

    if PHASES >= 3:
        tg1 = kb.dma("sync", BCA[:], ln0g_v.unsqueeze(0).broadcast_to((128, D)),
                     sem="dB")
        tg2 = kb.dma("sync", BCB[:], ln0b_v.unsqueeze(0).broadcast_to((128, D)),
                     waits=(tg1,), sem="dB")
        y_tok = {}
        prev_sc1 = None
        for tc in range(NT):
            g = tc % 8
            m_c, r_c, tr = ln_stats(XT[:, tc, :], g)
            wn = [tr, tg1, tg2]
            if prev_sc1:
                wn.append(prev_sc1)
            tn = kb.op("vector", lambda e, tc=tc, m=m_c, r=r_c: e.tensor_scalar(
                SC1[:], XT[:, tc, :], m, r, ALU.subtract, ALU.mult),
                waits=tuple(wn), inc="v")
            tgm = kb.op("gpsimd", lambda e: e.tensor_mul(SC1[:], SC1[:], BCA[:]),
                        waits=(tn,), inc="g")
            prev_sc1 = kb.op("vector", lambda e, tc=tc: e.tensor_add(
                YT[:, tc, :], SC1[:], BCB[:]), waits=(tgm,), inc="v")
            y_tok[tc] = prev_sc1
        t_park = kb.dma("sync", xpark, XT[:], waits=(kb.tok("g"), kb.tok("v")),
                        sem="dK")
        kb.barrier()

        rw_group_tok = {}
        sg_prev_tail = [None, None]
        sg_ps2_evac = [None]
        carry_zero = t_cc
        t_prev_wo_adds = None
        t_prev_rwkv = {}
        chain = {"t_k": None, "t_v": None, "t_sr": None, "t_rec": None,
                 "t_kv": None, "t_wk": None, "t_wkv": None, "t_t1": None,
                 "t_rwkv": None, "t_scank": None, "t_relu": None}
        hist = {"t1": [None, None], "rwkv": [None, None]}
        stage_tok = None
        t_mix_done = None

        def stream_w(srcs):
            nonlocal stage_tok
            toks = []
            for dsrc, dst, is_chunk, extra in srcs:
                w = [stage_tok] if stage_tok else []
                td = kb.dma("sync", SC1W[:] if is_chunk else SC1[:], dsrc,
                            waits=tuple(w), sem="dW")
                cw = [td] + list(extra)
                tc_ = kb.op("scalar", lambda e, dst=dst, is_chunk=is_chunk:
                            e.copy(dst, SC1W[:] if is_chunk else SC1[:]),
                            waits=tuple(cw), inc="s")
                stage_tok = tc_
                toks.append(tc_)
            return toks

        for h in range(2):
            if h == 1:
                kb.barrier()
            pst_state["tok"] = [kb.tok("s"), kb.tok("s")]
            if h == 0:
                tz = kb.op("vector", lambda e: e.memset(XNH[:, :, 0:1], 0.0),
                           waits=(t_park,), inc="v")
            else:
                tz = kb.op("vector", lambda e: e.tensor_copy(
                    XNH[:, :, 0:1], BND[:].unsqueeze(2)), inc="v")
            ev_last = None
            prev_pe = None
            for tcl in range(NC):
                tc = h * 8 + tcl
                g = tcl % 8
                m_c, r_c, tr = ln_stats(YT[:, tc, :], g,
                                        waits_v=(y_tok[tc],),
                                        waits_s=(y_tok[tc],))
                wn = [tr, tz]
                if prev_pe:
                    wn.append(prev_pe)
                tn = kb.op("vector", lambda e, tc=tc, m=m_c, r=r_c:
                           e.tensor_scalar(TMB[:], YT[:, tc, :], m, r,
                                           ALU.subtract, ALU.mult),
                           waits=tuple(wn), inc="v")
                prev_pe, ev_last = transpose_tile(
                    TMB[:], XNH[:, :, 1 + tcl * 128:1 + (tcl + 1) * 128], tn)
            taff = None
            for cc in range(NC):
                taff = kb.op("vector", lambda e, cc=cc: e.tensor_scalar(
                    XNH[:, cc, 1:1025], XNH[:, cc, 1:1025],
                    pcol("pre_g", cc), pcol("pre_b", cc), ALU.mult, ALU.add),
                    waits=(ev_last,) if cc == 0 else (), inc="v")
            tbnd = kb.op("vector", lambda e: e.tensor_copy(
                BND[:].unsqueeze(2), XNH[:, :, 1024:1025]), waits=(taff,),
                inc="v")
            tmix = None
            for cc in range(NC):
                xn_c = XNH[:, cc, 1:1025]
                xs_c = XNH[:, cc, 0:1024]
                for dst, mname in ((XK, "mix_k"), (XV, "mix_v"), (XR, "mix_r")):
                    mcol = pcol(mname, cc)
                    ta_ = kb.op("vector", lambda e, xs=xs_c, m=mcol:
                                e.scalar_tensor_tensor(TMB[:], xs, m, xs,
                                                       ALU.mult, ALU.subtract),
                                waits=(tbnd,) if tmix is None else (), inc="v")
                    tmix = kb.op("vector", lambda e, xn=xn_c, m=mcol, dst=dst,
                                 cc=cc: e.scalar_tensor_tensor(
                                     dst[:, cc, :], xn, m, TMB[:],
                                     ALU.mult, ALU.subtract), inc="v")
            t_mix_done = tmix
            if stage_tok is None:
                stage_tok = prev_sc1

            prev_pre = {"k": None, "v": None, "r": None}
            prev_cwo = None
            for c in range(NU):
                slot = c % 8
                BB2 = B2
                BB4 = B4
                BB5 = B5
                wo_extra = [t_mix_done] if c == 0 else (
                    [rw_group_tok[slot]] if slot in rw_group_tok else [])
                t_cwk = kb.dma("sync", WKB[:],
                               wkb_d[c].rearrange("p (c k) -> p c k", c=NC),
                               waits=tuple([precast["wk"][c]] +
                                           ([prev_pre["k"]] if prev_pre["k"]
                                            else [])), sem="dWk")
                t_cwv = kb.dma("sync", WVB[:],
                               wvb_d[c].rearrange("p (c k) -> p c k", c=NC),
                               waits=tuple([precast["wv"][c]] +
                                           ([prev_pre["v"]] if prev_pre["v"]
                                            else [])), sem="dWv")
                t_cwr = kb.dma("sync", WRB[:],
                               wrb_d[c].rearrange("p (c k) -> p c k", c=NC),
                               waits=tuple([precast["wr"][c]] +
                                           ([prev_pre["r"]] if prev_pre["r"]
                                            else [])), sem="dWr")
                t_cwo = kb.dma("sync", WOS[:, slot, :], wob_d[c],
                               waits=tuple([precast["wo"][c]] + wo_extra +
                                           ([prev_cwo] if prev_cwo else [])),
                               sem="dWo")
                prev_cwo = t_cwo

                def mm_pre(ps, wb, xb, waits):
                    tl = None
                    for d in range(NC):
                        for s2 in range(2):
                            tl = kb.op("tensor", lambda e, BB2=BB2, BB4=BB4, BB5=BB5, d=d, s2=s2, ps=ps,
                                       wb=wb, xb=xb: e.matmul(
                                           ps[:, s2 * 512:(s2 + 1) * 512],
                                           wb[:, d, :],
                                           xb[:, d, s2 * 512:(s2 + 1) * 512],
                                           start=(d == 0), stop=(d == NC - 1)),
                                       waits=tuple(waits)
                                       if (d == 0 and s2 == 0) else (),
                                       inc="t" if (d == NC - 1 and s2 == 1)
                                       else None)
                    return tl

                wk_w = [t_cwk, t_mix_done]
                if chain["t_relu"]:
                    wk_w.append(chain["t_relu"])
                t_kpre = mm_pre(PS0, WKB, XK, wk_w)
                wv_w = [t_cwv]
                if chain["t_v"]:
                    wv_w.append(chain["t_v"])
                t_vpre = mm_pre(PS1, WVB, XV, wv_w)
                wr_w = [t_cwr]
                if chain["t_sr"]:
                    wr_w.append(chain["t_sr"])
                if sg_ps2_evac[0] is not None:
                    wr_w.append(sg_ps2_evac[0])
                    sg_ps2_evac[0] = None
                t_rpre = mm_pre(PS2, WRB, XR, wr_w)
                prev_pre = {"k": t_kpre, "v": t_vpre, "r": t_rpre}

                w_relu = [t_kpre]
                t_relu = kb.op("scalar", lambda e, BB2=BB2, BB4=BB4, BB5=BB5: e.activation(
                    LNS[:], PS0[:], AF.Relu, bias=stc(C_60), scale=-1.0),
                    waits=tuple(w_relu), inc="s")
                w_k = []
                if chain["t_scank"]:
                    w_k.append(chain["t_scank"])
                if chain["t_kv"]:
                    w_k.append(chain["t_kv"])
                t_k = kb.op("scalar", lambda e, BB2=BB2, BB4=BB4, BB5=BB5: e.activation(
                    B1[:], LNS[:], AF.Exp, bias=stc(C_60), scale=-1.0),
                    waits=tuple(w_k), inc="s")
                w_v = [t_vpre]
                if chain["t_t1"]:
                    w_v.append(chain["t_t1"])
                t_v = kb.op("scalar", lambda e, BB2=BB2, BB4=BB4, BB5=BB5: e.copy(BB2[:], PS1[:]),
                            waits=tuple(w_v), inc="s")

                ck = ST[:, 64 + c:65 + c]
                ckv = ST[:, 96 + c:97 + c]
                dcl = DRV[:, DV["decay"] + c:DV["decay"] + c + 1]
                fcl = DRV[:, DV["first"] + c:DV["first"] + c + 1]
                w_sk = [t_k, carry_zero]
                if chain["t_rwkv"]:
                    w_sk.append(chain["t_rwkv"])
                t_scank = kb.op("vector", lambda e, BB2=BB2, BB4=BB4, BB5=BB5, d=dcl, ck=ck:
                                e.tensor_tensor_scan(
                                    BB4[:], d.broadcast_to((128, 1024)), B1[:],
                                    ck, ALU.mult, ALU.add),
                                waits=tuple(w_sk), inc="v")
                t_kv = kb.op("gpsimd", lambda e, BB2=BB2: e.tensor_mul(
                    B3[:], B1[:], BB2[:]), waits=(t_k, t_v), inc="g")
                kb.op("vector", lambda e, BB2=BB2, BB4=BB4, BB5=BB5, f=fcl: e.scalar_tensor_tensor(
                    BB2[:, 1:1024], B1[:, 1:1024], f, BB4[:, 0:1023],
                    ALU.mult, ALU.add), waits=(t_scank, t_kv), inc=None)
                kb.op("vector", lambda e, BB2=BB2, BB4=BB4, BB5=BB5, f=fcl, ck=ck: e.scalar_tensor_tensor(
                    BB2[:, 0:1], B1[:, 0:1], f, ck, ALU.mult, ALU.add))
                t_wk = kb.op("vector", lambda e, BB2=BB2, BB4=BB4, BB5=BB5, ck=ck: e.tensor_copy(
                    ck, BB4[:, 1023:1024]), inc="v")
                w_skv = []
                if chain["t_rwkv"]:
                    w_skv.append(chain["t_rwkv"])
                t_scankv = kb.op("vector", lambda e, BB2=BB2, BB4=BB4, BB5=BB5, d=dcl, ckv=ckv:
                                 e.tensor_tensor_scan(
                                     BB5[:], d.broadcast_to((128, 1024)), B3[:],
                                     ckv, ALU.mult, ALU.add),
                                 waits=tuple(w_skv), inc="v")
                kb.op("vector", lambda e, BB2=BB2, BB4=BB4, BB5=BB5, f=fcl: e.scalar_tensor_tensor(
                    BB4[:, 1:1024], B3[:, 1:1024], f, BB5[:, 0:1023],
                    ALU.mult, ALU.add))
                kb.op("vector", lambda e, BB2=BB2, BB4=BB4, BB5=BB5, f=fcl, ckv=ckv: e.scalar_tensor_tensor(
                    BB4[:, 0:1], B3[:, 0:1], f, ckv, ALU.mult, ALU.add))
                t_wkv = kb.op("vector", lambda e, BB2=BB2, BB4=BB4, BB5=BB5, ckv=ckv: e.tensor_copy(
                    ckv, BB5[:, 1023:1024]), inc="v")

                kb.op("scalar", lambda e, BB2=BB2, BB4=BB4, BB5=BB5: e.activation(
                    BB2[:], BB2[:], AF.Ln, bias=stc(C_KEPS)), waits=(t_wk,),
                    inc="s")
                t_rec = kb.op("scalar", lambda e, BB2=BB2, BB4=BB4, BB5=BB5: e.activation(
                    BB2[:], BB2[:], AF.Exp, scale=-1.0), inc="s")
                t_sr = kb.op("scalar", lambda e, BB2=BB2, BB4=BB4, BB5=BB5: e.activation(
                    BB5[:], PS2[:], AF.Sigmoid), waits=(t_rpre, t_wkv), inc="s")

                t_t1 = kb.op("gpsimd", lambda e, BB2=BB2, BB4=BB4:
                             e.tensor_mul(BB4[:], BB4[:], BB2[:]),
                             waits=(t_wkv, t_rec), inc="g")
                w_rw = [t_sr, t_t1]
                if slot in rw_group_tok:
                    w_rw.append(rw_group_tok[slot])
                t_rwkv = kb.op("gpsimd", lambda e, BB4=BB4, BB5=BB5,
                               slot=slot: e.tensor_mul(
                    RW[:, slot, :], BB4[:], BB5[:]), waits=tuple(w_rw),
                    inc="g")
                t_prev_rwkv[slot] = t_rwkv

                hist["t1"] = [t_t1, hist["t1"][0]]
                hist["rwkv"] = [t_rwkv, hist["rwkv"][0]]
                chain.update(t_k=t_k, t_v=t_v, t_sr=t_sr, t_rec=t_rec,
                             t_kv=t_kv, t_wk=t_wk, t_wkv=t_wkv, t_t1=t_t1,
                             t_rwkv=t_rwkv, t_scank=t_scank, t_relu=t_relu)

                if slot == 7:
                    gidx = c // 8
                    adds = []
                    evacs_sg = []
                    sg_last = None
                    for tl8 in range(8):
                        ps = PS3 if tl8 % 2 == 0 else PS2
                        wsg = [t_prev_rwkv[j] for j in range(8)]
                        wsg.append(t_cwo)
                        if tl8 == 1:
                            wsg.append(chain["t_sr"])
                        if tl8 >= 2:
                            wsg.append(evacs_sg[tl8 - 2])
                        tcg = h * 8 + tl8
                        tl_ = None
                        for j in range(8):
                            for os in range(2):
                                tl_ = kb.op("tensor", lambda e, j=j, os=os,
                                            ps=ps, tl8=tl8: e.matmul(
                                        ps[:, os * 512:(os + 1) * 512],
                                        RW[:, j, tl8 * 128:(tl8 + 1) * 128],
                                        WOS[:, j, os * 512:(os + 1) * 512],
                                        start=(j == 0), stop=(j == 7)),
                                    waits=tuple(wsg)
                                    if (j == 0 and os == 0) else (),
                                    inc="t" if (j == 7 and os == 1) else None)
                        sg_last = tl_
                        cv = SGA[:] if tl8 % 2 == 0 else SGB[:]
                        wev = [tl_]
                        if len(adds) >= 2:
                            wev.append(adds[tl8 - 2])
                        elif sg_prev_tail[1]:
                            wev.append(sg_prev_tail[0] if tl8 == 0
                                       else sg_prev_tail[1])
                        tev_ = kb.op("scalar", lambda e, ps=ps, cv=cv: e.copy(
                            cv[:], ps[:]), waits=tuple(wev), inc="s")
                        evacs_sg.append(tev_)
                        ta_ = kb.op("gpsimd", lambda e, cv=cv, tcg=tcg:
                                    e.tensor_add(YT[:, tcg, :], YT[:, tcg, :],
                                                 cv[:]), waits=(tev_,), inc="g")
                        adds.append(ta_)
                    for j in range(8):
                        rw_group_tok[j] = sg_last
                    sg_ps2_evac[0] = evacs_sg[-1]
                    sg_prev_tail = [adds[-2], adds[-1]]

        kb.barrier()

    if PHASES >= 4:
        prev = None
        t_cwr_done = None
        for cc in range(NC):
            td = kb.dma("sync", SC1[:], cwrT[:, cc, :],
                        waits=(prev,) if prev else (), sem="dW")
            t_cwr_done = kb.op("scalar", lambda e, cc=cc: e.copy(
                CWR[:, cc, :], SC1[:]), waits=(td,), inc="s")
            prev = t_cwr_done

        ps_free = [None, None, None]
        sig_tok = None
        xk2_tok = None
        for h in range(2):
            kb.barrier()
            pst_state["tok"] = [kb.tok("s"), kb.tok("s")]
            if h == 0:
                tz = kb.op("vector", lambda e: e.memset(XNH[:, :, 0:1], 0.0),
                           inc="v")
            else:
                tz = kb.op("vector", lambda e: e.tensor_copy(
                    XNH[:, :, 0:1], BND[:].unsqueeze(2)), inc="v")
            ev_last = None
            prev_pe = None
            for tcl in range(NC):
                tc = h * 8 + tcl
                g = tcl % 8
                m_c, r_c, tr = ln_stats(YT[:, tc, :], g)
                wn = [tr, tz]
                if prev_pe:
                    wn.append(prev_pe)
                tn = kb.op("vector", lambda e, tc=tc, m=m_c, r=r_c:
                           e.tensor_scalar(TMB[:], YT[:, tc, :], m, r,
                                           ALU.subtract, ALU.mult),
                           waits=tuple(wn), inc="v")
                prev_pe, ev_last = transpose_tile(
                    TMB[:], XNH[:, :, 1 + tcl * 128:1 + (tcl + 1) * 128], tn)
            taff = None
            for cc in range(NC):
                taff = kb.op("vector", lambda e, cc=cc: e.tensor_scalar(
                    XNH[:, cc, 1:1025], XNH[:, cc, 1:1025],
                    pcol("post_g", cc), pcol("post_b", cc), ALU.mult, ALU.add),
                    waits=(ev_last,) if cc == 0 else (), inc="v")
            tbnd = kb.op("vector", lambda e: e.tensor_copy(
                BND[:].unsqueeze(2), XNH[:, :, 1024:1025]), waits=(taff,),
                inc="v")
            tmix = None
            for cc in range(NC):
                xn_c = XNH[:, cc, 1:1025]
                xs_c = XNH[:, cc, 0:1024]
                for dst, mname in ((XK2[:, cc, h * 1024:(h + 1) * 1024],
                                    "cmix_k"),
                                   (XR2[:, cc, :], "cmix_r")):
                    mcol = pcol(mname, cc)
                    kb.op("vector", lambda e, xs=xs_c, m=mcol:
                          e.scalar_tensor_tensor(TMB[:], xs, m, xs,
                                                 ALU.mult, ALU.subtract),
                          waits=(tbnd,) if tmix is None else ())
                    tmix = kb.op("vector", lambda e, xn=xn_c, m=mcol, dst=dst:
                                 e.scalar_tensor_tensor(dst, xn, m, TMB[:],
                                                        ALU.mult,
                                                        ALU.subtract), inc="v")
            xk2_tok = tmix
            for tcl in range(NC):
                tc = h * 8 + tcl
                ps = PSF[tcl % 2]
                w = [tmix, t_cwr_done]
                if ps_free[tcl % 2]:
                    w.append(ps_free[tcl % 2])
                tl = None
                for d in range(NC):
                    for os in range(2):
                        tl = kb.op("tensor", lambda e, d=d, os=os, ps=ps,
                                   tcl=tcl: e.matmul(
                                       ps[:, os * 512:(os + 1) * 512],
                                       XR2[:, d, tcl * 128:(tcl + 1) * 128],
                                       CWR[:, d, os * 512:(os + 1) * 512],
                                       start=(d == 0), stop=(d == NC - 1)),
                                   waits=tuple(w) if (d == 0 and os == 0)
                                   else (),
                                   inc="t" if (d == NC - 1 and os == 1)
                                   else None)
                wsg = [tl]
                if sig_tok:
                    wsg.append(sig_tok)
                tsg = kb.op("scalar", lambda e, ps=ps: e.activation(
                    LNS[:], ps[:], AF.Sigmoid), waits=tuple(wsg), inc="s")
                ps_free[tcl % 2] = tsg
                sig_tok = kb.op("gpsimd", lambda e, tc=tc: e.tensor_add(
                    YT[:, tc, :], YT[:, tc, :], LNS[:]), waits=(tsg,), inc="g")

        kb.barrier()
        stage_tok2 = None
        silu_tok = {}
        g_group_tok = {}
        prev_adds_tail = [None, None, None]
        prev_k2 = None
        prev_cv4 = None
        sg_ps2_evac4 = [None]
        for c in range(NU):
            slot = c % 8
            wck = [precast["cwk"][c], xk2_tok]
            if prev_k2:
                wck.append(prev_k2)
            t_ck = kb.dma("sync", WKB[:],
                          cwkb_d[c].rearrange("p (c k) -> p c k", c=NC),
                          waits=tuple(wck), sem="dWk")
            wcv = [precast["cwv"][c]]
            if slot in g_group_tok:
                wcv.append(g_group_tok[slot])
            else:
                wcv.append(xk2_tok)
            if prev_cv4:
                wcv.append(prev_cv4)
            t_cv = kb.dma("sync", WOS[:, slot, :], cwvb_d[c],
                          waits=tuple(wcv), sem="dWo")
            prev_cv4 = t_cv

            wmm = [t_ck, xk2_tok]
            if c > 0:
                wmm.append(silu_tok[c - 1])
            tl = None
            for d in range(NC):
                for s4 in range(4):
                    ps = PS0 if s4 < 2 else PS1
                    col = (s4 % 2) * 512
                    tl = kb.op("tensor", lambda e, d=d, s4=s4, ps=ps, col=col:
                               e.matmul(ps[:, col:col + 512], WKB[:, d, :],
                                        XK2[:, d, s4 * 512:(s4 + 1) * 512],
                                        start=(d == 0), stop=(d == NC - 1)),
                               waits=tuple(wmm) if (d == 0 and s4 == 0) else (),
                               inc="t" if (d == NC - 1 and s4 == 3) else None)
            wsl = [tl]
            if slot in g_group_tok:
                wsl.append(g_group_tok[slot])
            prev_k2 = tl
            ts1_ = kb.op("scalar", lambda e, slot=slot: e.activation(
                GSL[:, slot, 0:1024], PS0[:], AF.Silu), waits=tuple(wsl),
                inc="s")
            silu_tok[c] = kb.op("scalar", lambda e, slot=slot: e.activation(
                GSL[:, slot, 1024:2048], PS1[:], AF.Silu), inc="s")

            if slot == 7:
                adds = []
                evacs_sg = []
                sg_last = None
                for tc in range(NT):
                    ps = PS3 if tc % 2 == 0 else PS2
                    wsg = [silu_tok[c]]
                    wsg.append(t_cv)
                    if tc >= 2:
                        wsg.append(evacs_sg[tc - 2])
                    tl_ = None
                    for j in range(8):
                        for os in range(2):
                            tl_ = kb.op("tensor", lambda e, j=j, os=os, ps=ps,
                                        tc=tc: e.matmul(
                                    ps[:, os * 512:(os + 1) * 512],
                                    GSL[:, j, tc * 128:(tc + 1) * 128],
                                    WOS[:, j, os * 512:(os + 1) * 512],
                                    start=(j == 0), stop=(j == 7)),
                                waits=tuple(wsg) if (j == 0 and os == 0)
                                else (),
                                inc="t" if (j == 7 and os == 1) else None)
                    sg_last = tl_
                    cv = SGA[:] if tc % 2 == 0 else SGB[:]
                    wev = [tl_]
                    if len(adds) >= 2:
                        wev.append(adds[tc - 2])
                    elif prev_adds_tail[2]:
                        wev.append(prev_adds_tail[1] if tc == 0
                                   else prev_adds_tail[2])
                    tev_ = kb.op("scalar", lambda e, ps=ps, cv=cv: e.copy(
                        cv, ps[:]), waits=tuple(wev), inc="s")
                    evacs_sg.append(tev_)
                    ta_ = kb.op("gpsimd", lambda e, cv=cv, tc=tc: e.tensor_add(
                        YT[:, tc, :], YT[:, tc, :], cv), waits=(tev_,),
                        inc="g")
                    adds.append(ta_)
                for j in range(8):
                    g_group_tok[j] = sg_last
                prev_adds_tail = [adds[-3], adds[-2], adds[-1]]
                sg_ps2_evac4[0] = evacs_sg[-1]

        kb.barrier()
        adds_fin = []
        EBS4 = [SC1, SC2, EB3, EB4]
        for tc in range(NT):
            wdma = [adds_fin[tc - 2]] if tc >= 2 else ()
            rtok = kb.dma("sync", XST[:, tc % 2, :], xpark[:, tc, :],
                          waits=tuple(wdma), sem="dT%d" % (tc % 2))
            g = tc % 8
            eb = EBS4[tc % 4]
            m_c, r_c, tr = ln_stats(YT[:, tc, :], g)
            wn = [tr]
            if tc >= 4:
                wn.append(adds_fin[tc - 4])
            tn = kb.op("vector", lambda e, tc=tc, m=m_c, r=r_c, eb=eb:
                       e.tensor_scalar(eb[:], YT[:, tc, :], m, r,
                                       ALU.subtract, ALU.mult),
                       waits=tuple(wn), inc="v")
            adds_fin.append(kb.op("gpsimd", lambda e, tc=tc, eb=eb:
                                  e.tensor_add(XT[:, tc, :], XST[:, tc % 2, :],
                                               eb[:]), waits=(tn, rtok),
                                  inc="g"))
        t_p4_sgevac = sg_ps2_evac4[0]
        t_p4_silu_last = silu_tok[NU - 1]

    if PHASES >= 5:
        tb = kb.dma("sync", BCA[:], bias_v.unsqueeze(0).broadcast_to((128, D)),
                    waits=(kb.tok("dB"),), sem="dB")
        prev = None
        t_wff = None
        for cc in range(NC):
            td = kb.dma("sync", ISTG[:], wffT[:, cc, :],
                        waits=(prev,) if prev else (), sem="dW")
            t_wff = kb.op("vector", lambda e, cc=cc: e.tensor_copy(
                WFF[:, cc, :], ISTG[:]), waits=(td,), inc="v")
            prev = t_wff

        pst_state["tok"] = [t_p4_sgevac, t_p4_sgevac]
        tq_tile = {}
        qrot5 = [adds_fin[12], adds_fin[13], adds_fin[14], adds_fin[15]]
        prev_round = None
        for tc in range(NT):
            g = tc % 8
            wv_t = (adds_fin[tc],)
            m_c, r_c, tr = ln_stats(
                XT[:, tc, :], g, waits_v=wv_t,
                waits_s=wv_t + ((prev_round,) if prev_round else ()),
                base=184)
            qb = EBS[tc % 4]
            wn = [tr]
            if qrot5[tc % 4]:
                wn.append(qrot5[tc % 4])
            tn = kb.op("vector", lambda e, tc=tc, m=m_c, r=r_c, qb=qb:
                       e.tensor_scalar(qb[:], XT[:, tc, :], m, r,
                                       ALU.subtract, ALU.mult),
                       waits=tuple(wn), inc="v")
            wsq = [tn]
            if prev_round:
                wsq.append(prev_round)
            tsq = kb.op("scalar", lambda e, qb=qb: e.activation(
                LNS[:], qb[:], AF.Square, accum_out=stc(130)), waits=tuple(wsq),
                inc="s")
            tr2a = kb.op("scalar", lambda e: e.activation(
                stc(131), stc(130), AF.Ln, bias=stc(C_EPSRMS),
                scale=1.0 / D), waits=(tsq,), inc="s")
            tr2 = kb.op("scalar", lambda e: e.activation(
                stc(131), stc(131), AF.Exp, scale=-0.5), inc="s")
            tam = kb.op("vector", lambda e, qb=qb: e.reduce_max(
                stc(132), qb[:], axis=AX.X, apply_absolute_value=True),
                waits=(tn,), inc="v")
            tax = kb.op("vector", lambda e: e.tensor_mul(
                stc(132), stc(132), stc(131)), waits=(tr2,), inc="v")
            tcl_ = kb.op("vector", lambda e: e.tensor_scalar_max(
                stc(132), stc(132), EPS_ACT), inc="v")
            kb.op("scalar", lambda e: e.activation(
                stc(133), stc(132), AF.Ln, scale=1.0 / 127.0), waits=(tcl_,))
            tlsr = kb.op("scalar", lambda e: e.activation(
                stc(133), stc(133), AF.Exp, scale=-1.0), inc="s")
            tr2s = kb.op("vector", lambda e: e.tensor_mul(
                stc(133), stc(133), stc(131)), waits=(tlsr,), inc="v")
            kb.op("vector", lambda e, tc=tc: e.tensor_scalar_mul(
                stc(24 + tc), stc(132), WSCALE / 127.0), inc="v")
            wq1 = [tr2s]
            if prev_round:
                wq1.append(prev_round)
            tq1 = kb.op("scalar", lambda e, qb=qb: e.activation(
                LNS[:], qb[:], AF.Copy, scale=stc(133)), waits=tuple(wq1),
                inc="s")
            qrot5[tc % 4] = tq1
            tq2 = kb.op("vector", lambda e, tc=tc: e.tensor_scalar(
                QTM[:, tc, :], LNS[:], RND, -RND, ALU.add, ALU.add),
                waits=(tq1,), inc="v")
            prev_round = tq2
            _, tq_tile[tc] = transpose_tile(
                QTM[:, tc, :], QFM[:, :, tc * 128:(tc + 1) * 128], tq2)

        ps_free = [t_p4_silu_last, t_p4_silu_last, t_p4_sgevac]
        add_toks = list(qrot5)
        for tc in range(NT):
            ps = PSF[tc % 3]
            w = [tq_tile[tc], t_wff]
            if ps_free[tc % 3]:
                w.append(ps_free[tc % 3])
            tl = None
            for d in range(NC):
                for os in range(2):
                    tl = kb.op("tensor", lambda e, d=d, os=os, ps=ps, tc=tc:
                               e.matmul(ps[:, os * 512:(os + 1) * 512],
                                        QFM[:, d, tc * 128:(tc + 1) * 128],
                                        WFF[:, d, os * 512:(os + 1) * 512],
                                        start=(d == 0), stop=(d == NC - 1)),
                               waits=tuple(w) if (d == 0 and os == 0) else (),
                               inc="t" if (d == NC - 1 and os == 1) else None)
            wn = [tl, tb]
            if add_toks[tc % 2]:
                wn.append(add_toks[tc % 2])
            eb = EBS[tc % 4]
            ta = kb.op("scalar", lambda e, ps=ps, tc=tc, eb=eb: e.activation(
                eb[:], ps[:], AF.Copy, scale=stc(24 + tc)), waits=tuple(wn),
                inc="s")
            ps_free[tc % 3] = ta
            tb2_ = kb.op("gpsimd", lambda e, eb=eb: e.tensor_add(
                eb[:], eb[:], BCA[:]), waits=(ta,), inc="g")
            g = tc % 8
            m_c, r_c, tr = ln_stats(eb[:], g, waits_v=(tb2_,), waits_s=(tb2_,))
            tn = kb.op("vector", lambda e, m=m_c, r=r_c, eb=eb: e.tensor_scalar(
                eb[:], eb[:], m, r, ALU.subtract, ALU.mult), waits=(tr,),
                inc="v")
            add_toks[tc % 4] = kb.op("gpsimd", lambda e, tc=tc, eb=eb:
                                     e.tensor_add(XT[:, tc, :], XT[:, tc, :],
                                                  eb[:]), waits=(tn,), inc="g")
            kb.dma("scalar", out_d[:, tc, :], XT[:, tc, :],
                   waits=(add_toks[tc % 4],), sem="o")
        kb.wait("sync", kb.tok("o"))
        kb.wait("gpsimd", kb.tok("o"))
    else:
        src = YT if PHASES == 3 else XT
        for tc in range(NT):
            kb.dma("scalar", out_d[:, tc, :], src[:, tc, :], sem="o")
        kb.wait("sync", kb.tok("o"))
        kb.wait("gpsimd", kb.tok("o"))

    kb.emit(nc)
    return nc


def kernel(**inputs):
    f32 = np.float32
    g = {k: np.asarray(v) for k, v in inputs.items()}

    x = np.ascontiguousarray(g["L0_attn_x"], dtype=f32)
    w_inT = _wpack(np.ascontiguousarray(g["L0_attn_W_in"].T))
    w_outT = _wpack(np.ascontiguousarray(g["L0_attn_W_out"].T))
    ffn0_rhs = _wpack(g["L0_ffn_rhs"])
    wk_u = _upack(g["L1_attn_tm_Wk"])
    wv_u = _upack(g["L1_attn_tm_Wv"])
    wr_u = _upack(g["L1_attn_tm_Wr"])
    wo_u = np.ascontiguousarray(np.asarray(g["L1_attn_tm_Wo"], dtype=f32)
                                .reshape(NU, 128, D))
    cwk_u = _upack(g["L1_attn_cm_Wk"])
    cwv_u = np.ascontiguousarray(np.asarray(g["L1_attn_cm_Wv"], dtype=f32)
                                 .reshape(NU, 128, D))
    cwrT = _wpack(g["L1_attn_cm_Wr"])
    wffT = _wpack(np.ascontiguousarray(g["L1_ffn_w_int8"].T).astype(np.int32))

    prm = np.zeros((128, 544), f32)
    col = 0
    for name, arr, w in [
        ("A_log", g["L0_attn_A_log"], 128),
        ("B_ssm", g["L0_attn_B_ssm"], 128),
        ("C_ssm", g["L0_attn_C_ssm"], 128),
        ("dt_log", g["L0_attn_dt_log"], 8),
        ("D_ssm", g["L0_attn_D_ssm"], 8),
        ("decay", g["L1_attn_tm_decay"], 32),
        ("first", g["L1_attn_tm_first"], 32),
        ("mix_k", g["L1_attn_tm_mix_k"], 8),
        ("mix_v", g["L1_attn_tm_mix_v"], 8),
        ("mix_r", g["L1_attn_tm_mix_r"], 8),
        ("cmix_k", g["L1_attn_cm_mix_k"], 8),
        ("cmix_r", g["L1_attn_cm_mix_r"], 8),
        ("pre_g", g["L1_attn_pre_g"], 8),
        ("pre_b", g["L1_attn_pre_b"], 8),
        ("post_g", g["L1_attn_post_g"], 8),
        ("post_b", g["L1_attn_post_b"], 8),
    ]:
        arr = np.asarray(arr, dtype=f32)
        if name in ("decay", "first"):
            pm = np.ascontiguousarray(arr.reshape(32, 128).T)
        else:
            pm = _pm(arr)
        prm[:, col:col + w] = pm
        col += w

    base = dict(
        w_inT=w_inT.astype(f32), w_outT=w_outT.astype(f32),
        ffn0_rhs=ffn0_rhs.astype(f32),
        wk_u=wk_u.astype(f32), wv_u=wv_u.astype(f32), wr_u=wr_u.astype(f32),
        wo_u=wo_u, cwk_u=cwk_u.astype(f32), cwv_u=cwv_u, cwrT=cwrT.astype(f32),
        wffT=wffT, prm=prm,
        ln0g_v=np.asarray(g["L1_attn_ln0_g"], f32),
        ln0b_v=np.asarray(g["L1_attn_ln0_b"], f32),
        bias_v=np.asarray(g["L1_ffn_bias"], f32),
        ident_b=np.eye(128).astype(ml_dtypes.bfloat16),
        ident_f=np.eye(128, dtype=f32),
    )

    nc = build_program(np.asarray(g["L1_ffn_weight_scale"], f32))
    in_maps = []
    for core in range(8):
        xc = np.ascontiguousarray(
            x[core].reshape(NT, 128, D).transpose(1, 0, 2))
        in_maps.append(dict(base, x=xc))
    res = run_bass_kernel_spmd(nc, in_maps, core_ids=list(range(8)))
    outs = []
    for core in range(8):
        o = res.results[core]["out"]
        outs.append(o.transpose(1, 0, 2).reshape(T, D))
    return np.stack(outs, axis=0).astype(np.float32)


if __name__ == "__main__":
    z = {k: np.zeros(s, dtype=d) for k, s, d in [("t", (1,), np.float32)]}
    print("kernel module loaded")


# version 18 (final)
# speedup vs baseline: 1.0177x; 1.0177x over previous
import os
import numpy as np
import ml_dtypes
import concourse.bass as bass
import concourse.mybir as mybir
from concourse.bass_utils import run_bass_kernel_spmd

F32 = mybir.dt.float32
BF16 = mybir.dt.bfloat16
I32 = mybir.dt.int32
AF = mybir.ActivationFunctionType
ALU = mybir.AluOpType
AX = mybir.AxisListType

ENGS = ("sync", "scalar", "vector", "tensor", "gpsimd")

T = 2048
D = 1024
NT = T // 128
NC = D // 128
NU = 32
NST = 16
EPS_LN = 1e-5
EPS_RMS = 1e-6
EPS_ACT = 1e-5
RND = 12582912.0
PHASES = int(os.environ.get("MK_PHASES", "5"))
SUB = int(os.environ.get("MK_SUB", "9"))


class KB:
    def __init__(self):
        self.streams = {e: [] for e in ENGS}
        self.cnt = {}

    AUTO = {"vector": "v", "scalar": "s", "gpsimd": "g"}

    def op(self, eng, fn, waits=(), inc=None, amt=1):
        waits = tuple((s, v) for (s, v) in
                      ((t for t in waits if t is not None))
                      if v is not None and v > 0)
        if eng in self.AUTO:
            ssem = self.AUTO[eng]
            sval = self.cnt.get(ssem, 0)
            if sval > 0:
                waits = waits + ((ssem, sval),)
            if inc is None:
                inc = ssem
        for s, _ in waits:
            self.cnt.setdefault(s, 0)
        if inc is not None:
            self.cnt.setdefault(inc, 0)
            self.cnt[inc] += amt
        self.streams[eng].append((fn, waits, inc, amt))
        return (inc, self.cnt[inc]) if inc else None

    def dma(self, q, out, in_, waits=(), sem=None, slow=False):
        semn = sem if sem is not None else {"sync": "dS", "scalar": "dC",
                                            "gpsimd": "dG", "vector": "dV"}[q]
        return self.op(q, lambda e, o=out, i=in_, sl=slow: e.dma_start(
            o, i, allow_slow_non_contiguous=sl) if sl else e.dma_start(o, i),
            waits=waits, inc=semn, amt=16)

    def wait(self, eng, *toks):
        toks = tuple((s, v) for (s, v) in
                     (t for t in toks if t is not None)
                     if s is not None and v is not None and v > 0)
        for s, _ in toks:
            self.cnt.setdefault(s, 0)
        if toks:
            self.streams[eng].append((None, toks, None, 0))

    def tok(self, semname):
        return (semname, self.cnt.get(semname, 0))

    def barrier(self):
        toks = tuple((s, v) for s, v in self.cnt.items() if v > 0)
        for e in ENGS:
            self.wait(e, *toks)

    def emit(self, nc):
        import contextlib
        with contextlib.ExitStack() as st:
            sems = {n: st.enter_context(nc.semaphore(n))
                    for n in sorted(self.cnt)}
            blk = st.enter_context(nc.Block())

            def mk(items):
                def f(e):
                    for fn, waits, inc, amt in items:
                        for s, v in waits:
                            e.wait_ge(sems[s], v)
                        if fn is not None:
                            ins = fn(e)
                            if inc is not None:
                                ins.then_inc(sems[inc], amt)
                return f
            for eng in ENGS:
                if self.streams[eng]:
                    getattr(blk, eng)(mk(self.streams[eng]))


def _pm(a):
    a = np.asarray(a)
    if a.ndim == 1:
        a = a[:, None]
    k = a.shape[1]
    return np.ascontiguousarray(
        a.reshape(NC, 128, k).transpose(1, 0, 2).reshape(128, NC * k))


def _wpack(w):
    w = np.asarray(w)
    return np.ascontiguousarray(w.reshape(NC, 128, w.shape[1]).transpose(1, 0, 2))


def _upack(w):
    w = np.asarray(w).reshape(NC, 128, NU, 128)
    return np.ascontiguousarray(w.transpose(2, 1, 0, 3))


def build_program(wscale_val):
    nc = bass.Bass("TRN2", target_bir_lowering=False, debug=False)
    kb = KB()

    def din(name, shape, dt=F32):
        return nc.dram_tensor(name, shape, dt, kind="ExternalInput").ap()

    x_in = din("x", [128, NT, D])
    w_inT = din("w_inT", [128, NC, D])
    w_outT = din("w_outT", [128, NC, D])
    ffn0_rhs = din("ffn0_rhs", [128, NC, D])
    wk_u = din("wk_u", [NU, 128, NC, 128])
    wv_u = din("wv_u", [NU, 128, NC, 128])
    wr_u = din("wr_u", [NU, 128, NC, 128])
    wo_u = din("wo_u", [NU, 128, D])
    cwk_u = din("cwk_u", [NU, 128, NC, 128])
    cwv_u = din("cwv_u", [NU, 128, D])
    cwrT = din("cwrT", [128, NC, D])
    wffT = din("wffT", [128, NC, D], I32)
    prm = din("prm", [128, 544])
    ln0g_v = din("ln0g_v", [D])
    ln0b_v = din("ln0b_v", [D])
    bias_v = din("bias_v", [D])
    ident_bd = din("ident_b", [128, 128], BF16)
    ident_fd = din("ident_f", [128, 128], F32)

    out_d = nc.dram_tensor("out", [128, NT, D], F32, kind="ExternalOutput").ap()
    xpark = nc.dram_tensor("xpark", [128, NT, D], F32, kind="Internal").ap()
    wkb_d = nc.dram_tensor("wkb_d", [NU, 128, D], BF16, kind="Internal").ap()
    wvb_d = nc.dram_tensor("wvb_d", [NU, 128, D], BF16, kind="Internal").ap()
    wrb_d = nc.dram_tensor("wrb_d", [NU, 128, D], BF16, kind="Internal").ap()
    wob_d = nc.dram_tensor("wob_d", [NU, 128, D], BF16, kind="Internal").ap()
    cwkb_d = nc.dram_tensor("cwkb_d", [NU, 128, D], BF16, kind="Internal").ap()
    cwvb_d = nc.dram_tensor("cwvb_d", [NU, 128, D], BF16, kind="Internal").ap()
    rs_rt = nc.dram_tensor("rs_rt", [2 * D], F32, kind="Internal").ap()

    WSCALE = float(wscale_val)

    PRM = {}
    off = 0
    for name, w in [("A_log", 128), ("B_ssm", 128), ("C_ssm", 128),
                    ("dt_log", 8), ("D_ssm", 8), ("decay", 32), ("first", 32),
                    ("mix_k", 8), ("mix_v", 8), ("mix_r", 8),
                    ("cmix_k", 8), ("cmix_r", 8), ("pre_g", 8), ("pre_b", 8),
                    ("post_g", 8), ("post_b", 8)]:
        PRM[name] = off
        off += w
    assert off <= 544

    SB_BASE = 0x4140

    def sb(name, offset, shape, dtype):
        return nc.alloc_sbuf_tensor_at(name, shape, dtype,
                                       offset=SB_BASE + offset)

    XT = sb("XT", 0x00000, [128, NT, D], F32)
    XK = sb("XK", 0x00000, [128, NC, 1024], BF16)
    XV = sb("XV", 0x04000, [128, NC, 1024], BF16)
    XR = sb("XR", 0x08000, [128, NC, 1024], BF16)
    RW = sb("RW", 0x0C000, [128, 8, 1024], BF16)
    XK2 = sb("XK2", 0x00000, [128, NC, T], BF16)
    XR2 = sb("XR2", 0x08000, [128, NC, 1024], BF16)
    CWR = sb("CWR", 0x0C000, [128, NC, D], BF16)
    GSL = sb("GSL", 0x08000, [128, 8, T], BF16)

    YT = sb("YT", 0x10000, [128, NT, D], F32)
    ZFM = sb("ZFM", 0x10000, [128, NC, T], BF16)
    YBC = sb("YBC", 0x10000, [128, NC, T], BF16)
    QTM = sb("QTM", 0x10000, [128, NT, D], BF16)
    XBF = sb("XBF", 0x18000, [128, NC, T], BF16)
    QFM = sb("QFM", 0x18000, [128, NC, T], BF16)

    WIN = sb("WIN", 0x20000, [128, NC, D], BF16)
    WOUT = sb("WOUT", 0x20000, [128, NC, D], BF16)
    QRHS = sb("QRHS", 0x20000, [128, NC, D], BF16)
    WFF = sb("WFF", 0x20000, [128, NC, D], BF16)
    XNH = sb("XNH", 0x20000, [128, NC, 1028], BF16)
    WOS = sb("WOS", 0x20000, [128, 8, D], BF16)
    YAC = sb("YAC", 0x24000, [128, T], F32)
    RSTG = sb("RSTG", 0x24000, [128, 2, D], F32)
    ISTG = sb("ISTG", 0x24000, [128, D], I32)
    HB = sb("HB", 0x26000, [128, T], F32)
    HBB = sb("HBB", 0x30000, [128, T], F32)
    HBC = sb("HBC", 0x2B960, [128, T], F32)
    WSTG3 = sb("WSTG3", 0x26000, [128, D], F32)
    SC2 = sb("SC2", 0x26000, [128, D], F32)
    EB3 = sb("EB3", 0x27000, [128, D], F32)
    EB4 = sb("EB4", 0x31000, [128, D], F32)
    B1 = sb("B1", 0x24100, [128, 1024], F32)
    B3 = sb("B3", 0x25100, [128, 1024], F32)
    B2 = sb("B2", 0x26100, [128, 1024], F32)
    B4 = sb("B4", 0x27100, [128, 1024], F32)
    B5 = sb("B5", 0x32000, [128, 1024], F32)
    SGA = sb("SGA", 0x30000, [128, 1024], F32)
    SGB = sb("SGB", 0x31000, [128, 1024], F32)
    RST1 = sb("RST1", 0x32000, [128, D], F32)
    WPS0 = sb("WPS0", 0x2C960, [128, NC, 128], F32)
    WPS1 = sb("WPS1", 0x30000, [128, NC, 128], F32)
    WPS2 = sb("WPS2", 0x31000, [128, NC, 128], F32)
    XIN2 = sb("XIN2", 0x20000, [128, 2, T], F32)

    ST = sb("ST", 0x28400, [128, 256], F32)
    PRMT = sb("PRMT", 0x28800, [128, 544], F32)
    DRV = sb("DRV", 0x29100, [128, 344], F32)
    IDB = sb("IDB", 0x29660, [128, 128], BF16)
    IDF = sb("IDF", 0x29760, [128, 128], F32)
    BCA = sb("BCA", 0x29960, [128, D], F32)
    BCB = sb("BCB", 0x2A960, [128, D], F32)
    LNS = sb("LNS", 0x2B960, [128, D], F32)
    TMB = sb("TMB", 0x2C160, [128, 1024], BF16)
    SC1 = sb("SC1", 0x2C960, [128, D], F32)
    SC1W = sb("SC1W", 0x2C960, [128, NC, 128], F32)
    WKB = sb("WKB", 0x2D960, [128, NC, 128], BF16)
    WVB = sb("WVB", 0x2E160, [128, NC, 128], BF16)
    WRB = sb("WRB", 0x2E960, [128, NC, 128], BF16)
    XST = sb("XST", 0x2D960, [128, 2, D], F32)
    BND = sb("BND", 0x2F960, [128, 8], BF16)
    CV1 = sb("CV1", 0x30000, [128, 1024], F32)
    CV2 = sb("CV2", 0x31000, [128, 1024], F32)
    assert SB_BASE + 0x32000 <= 229344

    DV = {}
    doff = 0
    for name, w in [("a_bar", 128), ("cb", 128), ("decay", 32), ("first", 32),
                    ("dt", 8), ("misc", 16)]:
        DV[name] = doff
        doff += w
    assert doff <= 344

    PS0 = nc.alloc_psum_tensor("PS0", [128, 1024], F32)
    PS1 = nc.alloc_psum_tensor("PS1", [128, 1024], F32)
    PS2 = nc.alloc_psum_tensor("PS2", [128, 1024], F32)
    PS3 = nc.alloc_psum_tensor("PS3", [128, 1024], F32)
    PSF = [PS0, PS1, PS2]

    C_EPSLN, C_EPSRMS, C_EPSACT, C_60, C_KEPS = 240, 241, 242, 243, 244

    def stc(i):
        return ST[:, i:i + 1]

    def pcol(name, j=0):
        o = PRM[name] + j
        return PRMT[:, o:o + 1]

    def dvcol(name, j=0):
        o = DV[name] + j
        return DRV[:, o:o + 1]

    def ln_stats(src_ap, g, waits_v=(), waits_s=(), eps_col=C_EPSLN,
                 base=136):
        b = base + 6 * (g % 8)
        s_c, q_c, m_c, n_c, r_c = (stc(b), stc(b + 1), stc(b + 2),
                                   stc(b + 3), stc(b + 4))
        kb.op("vector", lambda e: e.reduce_sum(s_c, src_ap, axis=AX.X),
              waits=waits_v)
        kb.op("vector", lambda e: e.tensor_scalar_mul(m_c, s_c, 1.0 / D))
        tv = kb.op("vector", lambda e: e.tensor_mul(n_c, m_c, m_c), inc="v")
        ts = kb.op("scalar", lambda e: e.activation(LNS[:], src_ap, AF.Square,
                                                    accum_out=q_c),
                   waits=waits_s, inc="s")
        tv2 = kb.op("vector", lambda e: e.scalar_tensor_tensor(
            r_c, q_c, 1.0 / D, n_c, ALU.mult, ALU.subtract),
            waits=(ts, tv), inc="v")
        tsq = kb.op("scalar", lambda e: e.activation(
            r_c, r_c, AF.Ln, bias=stc(eps_col)), waits=(tv2,), inc="s")
        tr = kb.op("scalar", lambda e: e.activation(
            r_c, r_c, AF.Exp, scale=-0.5), inc="s")
        return m_c, r_c, tr

    pst_state = {"half": 0, "tok": [None, None]}

    def transpose_tile(src, dst_sl, wait_src):
        h = pst_state["half"]
        base = h * 512
        last = None
        for j in range(NC):
            w = []
            if j == 0:
                w = [wait_src, pst_state["tok"][h]]
            last = kb.op("tensor", lambda e, j=j, base=base: e.matmul(
                PS3[:, base + j * 64: base + (j + 1) * 64].bitcast(BF16),
                src[:, j * 128:(j + 1) * 128], IDB[:],
                start=True, stop=True, is_transpose=True),
                waits=w, inc="t" if j == NC - 1 else None)
        ev = kb.op("scalar", lambda e, base=base: e.copy(
            dst_sl, PS3[:, base:base + 512].bitcast(BF16).rearrange(
                "p (c t) -> p c t", c=NC)), waits=(last,), inc="s")
        pst_state["tok"][h] = ev
        pst_state["half"] = 1 - h
        return last, ev

    t_x = kb.dma("sync", XT[:], x_in, sem="dX")
    t_prm = kb.dma("scalar", PRMT[:], prm, sem="dP")
    t_idb = kb.dma("scalar", IDB[:], ident_bd, sem="dP")
    t_idf = kb.dma("scalar", IDF[:], ident_fd, sem="dP")
    t_prm = t_idb = t_idf = kb.tok("dP")
    for col, val in [(C_EPSLN, EPS_LN), (C_EPSRMS, EPS_RMS),
                     (C_EPSACT, EPS_ACT), (C_60, 60.0), (C_KEPS, 1e-8)]:
        kb.op("vector", lambda e, c=col, v=val: e.memset(stc(c), v))
    t_cc = kb.op("vector", lambda e: e.memset(ST[:, 64:128], 0.0), inc="v")
    nc.const_aps.register(F32, 1e-8, stc(C_KEPS))

    prev = t_prm
    t_wincast = None
    for c in range(NC):
        td = kb.dma("sync", SC1[:], w_inT[:, c, :], waits=(prev,), sem="dW")
        t_wincast = kb.op("scalar", lambda e, c=c: e.copy(WIN[:, c, :], SC1[:]),
                          waits=(td,), inc="s")
        prev = t_wincast

    AL = PRMT[:, PRM["A_log"]:PRM["A_log"] + 128]
    BS = PRMT[:, PRM["B_ssm"]:PRM["B_ssm"] + 128]
    CS = PRMT[:, PRM["C_ssm"]:PRM["C_ssm"] + 128]
    ABAR = DRV[:, DV["a_bar"]:DV["a_bar"] + 128]
    CBV = DRV[:, DV["cb"]:DV["cb"] + 128]
    TDC = DRV[:, DV["decay"]:DV["decay"] + 32]
    TFC = DRV[:, DV["first"]:DV["first"] + 32]
    DTC = DRV[:, DV["dt"]:DV["dt"] + 8]
    kb.wait("scalar", t_prm)
    texpa = kb.op("scalar", lambda e: e.activation(ABAR, AL, AF.Exp), inc="s")
    tdt = kb.op("scalar", lambda e: e.activation(
        DTC, PRMT[:, PRM["dt_log"]:PRM["dt_log"] + 8], AF.Exp), inc="s")
    ts0 = kb.op("vector", lambda e: e.reciprocal(CBV, ABAR), waits=(texpa,),
                inc="v")
    tv0 = None
    for c in range(NC):
        tv0 = kb.op("vector", lambda e, c=c: e.tensor_scalar(
            ABAR[:, c * NST:(c + 1) * NST], ABAR[:, c * NST:(c + 1) * NST],
            DRV[:, DV["dt"] + c:DV["dt"] + c + 1], -1.0, ALU.mult, ALU.mult),
            waits=(ts0, tdt) if c == 0 else (), inc="v")
    ts1 = kb.op("scalar", lambda e: e.activation(ABAR, ABAR, AF.Exp),
                waits=(tv0,), inc="s")
    tv1 = kb.op("vector", lambda e: e.scalar_tensor_tensor(
        CBV, ABAR, 1.0, CBV, ALU.subtract, ALU.mult), waits=(ts1,), inc="v")
    tv2 = kb.op("vector", lambda e: e.tensor_mul(CBV, CBV, BS), inc="v")
    t_cb = kb.op("vector", lambda e: e.scalar_tensor_tensor(
        CBV, CBV, -1.0, CS, ALU.mult, ALU.mult), inc="v")
    ts2 = kb.op("scalar", lambda e: e.activation(
        TDC, PRMT[:, PRM["decay"]:PRM["decay"] + 32], AF.Exp), inc="s")
    ts3 = kb.op("scalar", lambda e: e.activation(TDC, TDC, AF.Exp, scale=-1.0),
                waits=(ts2,), inc="s")
    t_first = kb.op("scalar", lambda e: e.activation(
        TFC, PRMT[:, PRM["first"]:PRM["first"] + 32], AF.Exp), inc="s")

    kb.barrier()

    prev_pe = None
    for tc in range(NT if SUB >= 1 else 0):
        g = tc % 8
        m_c, r_c, tr = ln_stats(XT[:, tc, :], g, waits_v=(t_x,) if tc == 0 else (),
                                waits_s=(t_x,) if tc == 0 else ())
        wn = [tr]
        if prev_pe:
            wn.append(prev_pe)
        norm_tok = kb.op("vector", lambda e, tc=tc, m=m_c, r=r_c: e.tensor_scalar(
            TMB[:], XT[:, tc, :], m, r, ALU.subtract, ALU.mult),
            waits=tuple(wn), inc="v")
        prev_pe, _ = transpose_tile(
            TMB[:], ZFM[:, :, tc * 128:(tc + 1) * 128], norm_tok)
    t_zdone = kb.tok("s")

    t_evac = {}
    t_mm_last = None
    for ec in range(NC if SUB >= 2 else 0):
        w = [kb.tok("s")] if ec == 0 else [t_evac[ec - 1][1]]
        for d in range(NC):
            for s4 in range(4):
                ps = PS0 if s4 < 2 else PS1
                col = (s4 % 2) * 512
                t_mm_last = kb.op("tensor", lambda e, d=d, s4=s4, ps=ps, col=col,
                                  ec=ec: e.matmul(
                    ps[:, col:col + 512], WIN[:, d, ec * 128:(ec + 1) * 128],
                    ZFM[:, d, s4 * 512:(s4 + 1) * 512],
                    start=(d == 0), stop=(d == NC - 1)),
                    waits=tuple(w) if (d == 0 and s4 == 0) else (),
                    inc="t" if (d == NC - 1 and s4 == 3) else None)
        e1 = kb.op("scalar", lambda e, ec=ec: e.copy(XBF[:, ec, 0:1024], PS0[:]),
                   waits=(t_mm_last,), inc="s")
        e2 = kb.op("scalar", lambda e, ec=ec: e.copy(XBF[:, ec, 1024:2048], PS1[:]),
                   inc="s")
        t_evac[ec] = (e1, e2)
    t_win_mm_done = t_mm_last


    precast = {}
    wprep_items = []
    if PHASES >= 3:
        stg_in = [WPS0, WPS1, WPS2]
        stg_bf = [WKB, WVB, WRB]
        srcs_all = [("wk", wk_u, wkb_d, True), ("wv", wv_u, wvb_d, True),
                    ("wr", wr_u, wrb_d, True), ("wo", wo_u, wob_d, False),
                    ("cwk", cwk_u, cwkb_d, True), ("cwv", cwv_u, cwvb_d, False)]
        for kind, dsrc, ddst, is_chunk in srcs_all:
            precast[kind] = {}
            for c in range(NU):
                wprep_items.append((kind, c, dsrc, ddst))

    wprep_state = {"i": 0, "cast_tok": [t_wincast, None, None],
                   "out_tok": [None, None, None]}

    def emit_wprep(nmax):
        for _ in range(nmax):
            i = wprep_state["i"]
            if i >= len(wprep_items):
                return
            wprep_state["i"] = i + 1
            kind, c, dsrc, ddst = wprep_items[i]
            s3 = i % 3
            sin = [WPS0, WPS1, WPS2][s3]
            sbf = [WKB, WVB, WRB][s3]
            w_in = [wprep_state["cast_tok"][s3]]
            tdi = kb.dma("sync", sin[:], dsrc[c].rearrange(
                "p (c k) -> p c k", c=NC) if dsrc[c].shape == (128, D)
                else dsrc[c], waits=tuple(t for t in w_in if t),
                sem="dWp%d" % s3)
            wcast = [tdi]
            if wprep_state["out_tok"][s3]:
                wcast.append(wprep_state["out_tok"][s3])
            tc_ = kb.op("scalar", lambda e, sin=sin, sbf=sbf: e.copy(
                sbf[:], sin[:]), waits=tuple(wcast), inc="s")
            wprep_state["cast_tok"][s3] = tc_
            tdo = kb.dma("scalar", ddst[c].rearrange("p (c k) -> p c k", c=NC),
                         sbf[:], waits=(tc_,), sem="dWq%d" % s3)
            wprep_state["out_tok"][s3] = tdo
            precast[kind][c] = tdo

    if PHASES >= 2:
        tred = None
        slot_use_r = None
        for c in range(NC):
            wdma = [slot_use_r] if slot_use_r else []
            tdma = kb.dma("sync", RST1[:], ffn0_rhs[:, c, :],
                          waits=tuple(wdma), sem="dR0")
            tt = None
            last_red = [None, None]
            for j in range(NC):
                h2 = j % 2
                po = h2 * 512
                wt = [tdma, t_idf]
                if last_red[h2]:
                    wt.append(last_red[h2])
                elif tred:
                    wt.append(tred)
                tt = kb.op("tensor", lambda e, c=c, j=j, po=po: e.matmul(
                    PS2[:, po:po + 128], RST1[:, j * 128:(j + 1) * 128],
                    IDF[:], start=True, stop=True, is_transpose=True),
                    waits=tuple(wt), inc="t")
                kb.op("vector", lambda e, po=po, h2=h2: e.reduce_max(
                    stc(32 + h2), PS2[:, po:po + 128], axis=AX.X,
                    apply_absolute_value=True), waits=(tt,))
                if c == 0:
                    tred = kb.op("vector", lambda e, j=j, h2=h2: e.tensor_copy(
                        stc(40 + j), stc(32 + h2)), inc="v")
                else:
                    tred = kb.op("vector", lambda e, j=j, h2=h2: e.tensor_max(
                        stc(40 + j), stc(40 + j), stc(32 + h2)), inc="v")
                last_red[h2] = tred
            slot_use_r = tt
        t_absmax_done = tred
        tcl = kb.op("vector", lambda e: e.tensor_scalar_max(
            ST[:, 40:48], ST[:, 40:48], 1e-12), waits=(tred,), inc="v")
        kb.op("scalar", lambda e: e.activation(
            ST[:, 48:56], ST[:, 40:48], AF.Ln, scale=1.0 / 127.0),
            waits=(tcl,))
        trc = kb.op("scalar", lambda e: e.activation(
            ST[:, 48:56], ST[:, 48:56], AF.Exp, scale=-1.0), inc="s")
        tds = kb.op("vector", lambda e: e.tensor_scalar_mul(
            ST[:, 56:64], ST[:, 40:48], 1.0 / 127.0), waits=(tcl,), inc="v")
        tp1 = kb.dma("sync", rs_rt[0:D].rearrange("(c p) -> p c", p=128),
                     ST[:, 48:56], waits=(trc,), slow=True, sem="dB")
        tp2 = kb.dma("sync", rs_rt[D:2 * D].rearrange("(c p) -> p c", p=128),
                     ST[:, 56:64], waits=(tds, tp1), slow=True, sem="dB")
        tb1 = kb.dma("sync", BCB[:],
                     rs_rt[0:D].unsqueeze(0).broadcast_to((128, D)),
                     waits=(tp2,), sem="dB")
        tb2 = kb.dma("sync", BCA[:],
                     rs_rt[D:2 * D].unsqueeze(0).broadcast_to((128, D)),
                     waits=(tb1,), sem="dB")

    t_cast_y = {}
    scan_last = {}
    for c in range(NC if SUB >= 3 else 0):
        wcv = [t_evac[c][1]]
        if c >= 2:
            wcv.append(scan_last[c - 2])
        t_cvt = kb.op("scalar", lambda e, c=c: e.copy(
            XIN2[:, c % 2, :], XBF[:, c, :]), waits=tuple(wcv), inc="s")
        wv = [t_cvt, t_cb]
        if c > 0:
            wv.append(t_cast_y[c - 1])
        tini = kb.op("vector", lambda e, c=c: e.tensor_scalar_mul(
            YAC[:], XIN2[:, c % 2, :], pcol("D_ssm", c)), waits=tuple(wv),
            inc="v")
        last_add = None
        for n in range(NST):
            acol = DRV[:, DV["a_bar"] + c * NST + n:DV["a_bar"] + c * NST + n + 1]
            ccol = DRV[:, DV["cb"] + c * NST + n:DV["cb"] + c * NST + n + 1]
            kb.op("vector", lambda e, c=c, a=acol:
                  e.tensor_tensor_scan(
                      HB[:, 0:1024], a.broadcast_to((128, 1024)),
                      XIN2[:, c % 2, 0:1024], 0.0, ALU.mult, ALU.add))
            tsc = kb.op("vector", lambda e, c=c, a=acol:
                        e.tensor_tensor_scan(
                            HB[:, 1024:2048], a.broadcast_to((128, 1024)),
                            XIN2[:, c % 2, 1024:2048], HB[:, 1023:1024],
                            ALU.mult, ALU.add), inc="v")
            scan_last[c] = tsc
            last_add = kb.op("vector", lambda e, cc=ccol: e.scalar_tensor_tensor(
                YAC[:], HB[:], cc, YAC[:], ALU.mult, ALU.add), inc="v")
        wv2 = [last_add, t_win_mm_done]
        t_cast_y[c] = kb.op("scalar", lambda e, c=c: e.copy(YBC[:, c, :], YAC[:]),
                            waits=tuple(wv2), inc="s")
        emit_wprep(26)

    emit_wprep(10 ** 6)
    t_wprep_done = kb.tok("s")

    kb.wait("sync", kb.tok("g"))
    prev = kb.tok("v")
    t_woutcast = None
    for c in range(NC if SUB >= 4 else 0):
        td = kb.dma("sync", WSTG3[:], w_outT[:, c, :], waits=(prev,), sem="dW")
        t_woutcast = kb.op("scalar", lambda e, c=c: e.copy(WOUT[:, c, :],
                                                           WSTG3[:]),
                           waits=(td,), inc="s")
        prev = t_woutcast

    ps_free = [None, None, None]
    add_toks = [None, None, None, None]
    add_p1 = {}
    EBS = [SC1, SC2, EB3, EB4]
    if PHASES >= 2:
        ps_free[2] = t_absmax_done
    for tc in range(NT if SUB >= 4 else 0):
        ps = PSF[tc % 3]
        w = [t_cast_y[NC - 1], t_woutcast]
        if ps_free[tc % 3]:
            w.append(ps_free[tc % 3])
        tl = None
        for ecc in range(NC):
            for os in range(2):
                tl = kb.op("tensor", lambda e, ecc=ecc, os=os, ps=ps, tc=tc:
                           e.matmul(ps[:, os * 512:(os + 1) * 512],
                                    YBC[:, ecc, tc * 128:(tc + 1) * 128],
                                    WOUT[:, ecc, os * 512:(os + 1) * 512],
                                    start=(ecc == 0), stop=(ecc == NC - 1)),
                           waits=tuple(w) if (ecc == 0 and os == 0) else (),
                           inc="t" if (ecc == NC - 1 and os == 1) else None)
        eb = EBS[tc % 4]
        wev = [tl]
        if add_toks[tc % 4]:
            wev.append(add_toks[tc % 4])
        tev = kb.op("scalar", lambda e, ps=ps, eb=eb: e.copy(eb[:], ps[:]),
                    waits=tuple(wev), inc="s")
        ps_free[tc % 3] = tev
        g = tc % 8
        m_c, r_c, tr = ln_stats(eb[:], g, waits_v=(tev,), waits_s=(tev,))
        tn = kb.op("vector", lambda e, m=m_c, r=r_c, eb=eb: e.tensor_scalar(
            eb[:], eb[:], m, r, ALU.subtract, ALU.mult), waits=(tr,),
            inc="v")
        add_toks[tc % 4] = kb.op("gpsimd", lambda e, tc=tc, eb=eb: e.tensor_add(
            XT[:, tc, :], XT[:, tc, :], eb[:]), waits=(tn,), inc="g")
        add_p1[tc] = add_toks[tc % 4]

    t_p1_mm_done = kb.tok("t")
    if PHASES >= 2:

        tq_tile = {}
        qrot = list(add_toks)
        prev_round = None
        for tc in range(NT):
            g = tc % 8
            qb = EBS[tc % 4]
            wv_t = [add_p1[tc]] if tc in add_p1 else []
            m_c, r_c, tr = ln_stats(
                XT[:, tc, :], g, waits_v=tuple(wv_t),
                waits_s=tuple(wv_t) + ((prev_round,) if prev_round else ()),
                base=184)
            wn = [tr]
            if qrot[tc % 4]:
                wn.append(qrot[tc % 4])
            tn = kb.op("vector", lambda e, tc=tc, m=m_c, r=r_c, qb=qb:
                       e.tensor_scalar(qb[:], XT[:, tc, :], m, r,
                                       ALU.subtract, ALU.mult),
                       waits=tuple(wn), inc="v")
            tam = kb.op("vector", lambda e, qb=qb: e.reduce_max(
                stc(128), qb[:], axis=AX.X, apply_absolute_value=True),
                inc="v")
            tcl2 = kb.op("vector", lambda e: e.tensor_scalar_max(
                stc(128), stc(128), 1e-12), inc="v")
            kb.op("scalar", lambda e: e.activation(
                stc(129), stc(128), AF.Ln, scale=1.0 / 127.0), waits=(tcl2,))
            tlsr = kb.op("scalar", lambda e: e.activation(
                stc(129), stc(129), AF.Exp, scale=-1.0), inc="s")
            kb.op("vector", lambda e, tc=tc: e.tensor_scalar_mul(
                stc(24 + tc), stc(128), 1.0 / 127.0), inc="v")
            wq1 = [tlsr]
            if prev_round:
                wq1.append(prev_round)
            tq1 = kb.op("scalar", lambda e, qb=qb: e.activation(
                LNS[:], qb[:], AF.Copy, scale=stc(129)), waits=tuple(wq1),
                inc="s")
            qrot[tc % 4] = tq1
            tq2 = kb.op("vector", lambda e, tc=tc: e.tensor_scalar(
                QTM[:, tc, :], LNS[:], RND, -RND, ALU.add, ALU.add),
                waits=(tq1,), inc="v")
            prev_round = tq2
            _, tq_tile[tc] = transpose_tile(
                QTM[:, tc, :], QFM[:, :, tc * 128:(tc + 1) * 128], tq2)

        tqr = None
        slot_use_r2 = kb.tok("t")
        for c in range(NC):
            wdma = [tb2, slot_use_r2]
            if tqr:
                wdma.append(tqr)
            tdma = kb.dma("sync", RST1[:], ffn0_rhs[:, c, :],
                          waits=tuple(wdma), sem="dR0")
            wq = [tdma, tb1]
            if c == 0:
                wq.append(t_wprep_done)
            t1 = kb.op("vector", lambda e, c=c: e.tensor_mul(
                HBB[:, 0:1024], RST1[:], BCB[:]), waits=tuple(wq), inc="v")
            tqr = kb.op("vector", lambda e, c=c: e.tensor_scalar(
                QRHS[:, c, :], HBB[:, 0:1024], RND, -RND, ALU.add, ALU.add),
                waits=(t_p1_mm_done,) if c == 0 else (), inc="v")
        t_qrhs = tqr

        add_toks = list(qrot)
        for tc in range(NT):
            ps = PSF[tc % 3]
            w = [tq_tile[tc], t_qrhs]
            if ps_free[tc % 3]:
                w.append(ps_free[tc % 3])
            tl = None
            for d in range(NC):
                for os in range(2):
                    tl = kb.op("tensor", lambda e, d=d, os=os, ps=ps, tc=tc:
                               e.matmul(ps[:, os * 512:(os + 1) * 512],
                                        QFM[:, d, tc * 128:(tc + 1) * 128],
                                        QRHS[:, d, os * 512:(os + 1) * 512],
                                        start=(d == 0), stop=(d == NC - 1)),
                               waits=tuple(w) if (d == 0 and os == 0) else (),
                               inc="t" if (d == NC - 1 and os == 1) else None)
            wn = [tl, tb2]
            if add_toks[tc % 4]:
                wn.append(add_toks[tc % 4])
            eb = EBS[tc % 4]
            ta = kb.op("scalar", lambda e, ps=ps, tc=tc, eb=eb: e.activation(
                eb[:], ps[:], AF.Copy, scale=stc(24 + tc)), waits=tuple(wn),
                inc="s")
            ps_free[tc % 3] = ta
            tb = kb.op("gpsimd", lambda e, eb=eb: e.tensor_mul(
                eb[:], eb[:], BCA[:]), waits=(ta,), inc="g")
            g = tc % 8
            m_c, r_c, tr = ln_stats(eb[:], g, waits_v=(tb,), waits_s=(tb,))
            tn = kb.op("vector", lambda e, m=m_c, r=r_c, eb=eb: e.tensor_scalar(
                eb[:], eb[:], m, r, ALU.subtract, ALU.mult), waits=(tr,),
                inc="v")
            add_toks[tc % 4] = kb.op("gpsimd", lambda e, tc=tc, eb=eb:
                                     e.tensor_add(XT[:, tc, :], XT[:, tc, :],
                                                  eb[:]), waits=(tn,), inc="g")

        kb.barrier()

    if PHASES >= 3:
        tg1 = kb.dma("sync", BCA[:], ln0g_v.unsqueeze(0).broadcast_to((128, D)),
                     sem="dB")
        tg2 = kb.dma("sync", BCB[:], ln0b_v.unsqueeze(0).broadcast_to((128, D)),
                     waits=(tg1,), sem="dB")
        y_tok = {}
        prev_sc1 = None
        for tc in range(NT):
            g = tc % 8
            m_c, r_c, tr = ln_stats(XT[:, tc, :], g)
            wn = [tr, tg1, tg2]
            if prev_sc1:
                wn.append(prev_sc1)
            tn = kb.op("vector", lambda e, tc=tc, m=m_c, r=r_c: e.tensor_scalar(
                SC1[:], XT[:, tc, :], m, r, ALU.subtract, ALU.mult),
                waits=tuple(wn), inc="v")
            tgm = kb.op("gpsimd", lambda e: e.tensor_mul(SC1[:], SC1[:], BCA[:]),
                        waits=(tn,), inc="g")
            prev_sc1 = kb.op("vector", lambda e, tc=tc: e.tensor_add(
                YT[:, tc, :], SC1[:], BCB[:]), waits=(tgm,), inc="v")
            y_tok[tc] = prev_sc1
        t_park = kb.dma("sync", xpark, XT[:], waits=(kb.tok("g"), kb.tok("v")),
                        sem="dK")
        kb.barrier()

        rw_group_tok = {}
        sg_prev_tail = [None, None]
        sg_ps2_evac = [None]
        carry_zero = t_cc
        t_prev_wo_adds = None
        t_prev_rwkv = {}
        chain = {"t_k": None, "t_v": None, "t_sr": None, "t_rec": None,
                 "t_kv": None, "t_wk": None, "t_wkv": None, "t_t1": None,
                 "t_rwkv": None, "t_scank": None, "t_relu": None}
        hist = {"t1": [None, None], "rwkv": [None, None]}
        stage_tok = None
        t_mix_done = None

        def stream_w(srcs):
            nonlocal stage_tok
            toks = []
            for dsrc, dst, is_chunk, extra in srcs:
                w = [stage_tok] if stage_tok else []
                td = kb.dma("sync", SC1W[:] if is_chunk else SC1[:], dsrc,
                            waits=tuple(w), sem="dW")
                cw = [td] + list(extra)
                tc_ = kb.op("scalar", lambda e, dst=dst, is_chunk=is_chunk:
                            e.copy(dst, SC1W[:] if is_chunk else SC1[:]),
                            waits=tuple(cw), inc="s")
                stage_tok = tc_
                toks.append(tc_)
            return toks

        for h in range(2):
            if h == 1:
                kb.barrier()
            pst_state["tok"] = [kb.tok("s"), kb.tok("s")]
            if h == 0:
                tz = kb.op("vector", lambda e: e.memset(XNH[:, :, 0:1], 0.0),
                           waits=(t_park,), inc="v")
            else:
                tz = kb.op("vector", lambda e: e.tensor_copy(
                    XNH[:, :, 0:1], BND[:].unsqueeze(2)), inc="v")
            ev_last = None
            prev_pe = None
            for tcl in range(NC):
                tc = h * 8 + tcl
                g = tcl % 8
                m_c, r_c, tr = ln_stats(YT[:, tc, :], g,
                                        waits_v=(y_tok[tc],),
                                        waits_s=(y_tok[tc],))
                wn = [tr, tz]
                if prev_pe:
                    wn.append(prev_pe)
                tn = kb.op("vector", lambda e, tc=tc, m=m_c, r=r_c:
                           e.tensor_scalar(TMB[:], YT[:, tc, :], m, r,
                                           ALU.subtract, ALU.mult),
                           waits=tuple(wn), inc="v")
                prev_pe, ev_last = transpose_tile(
                    TMB[:], XNH[:, :, 1 + tcl * 128:1 + (tcl + 1) * 128], tn)
            taff = None
            for cc in range(NC):
                taff = kb.op("vector", lambda e, cc=cc: e.tensor_scalar(
                    XNH[:, cc, 1:1025], XNH[:, cc, 1:1025],
                    pcol("pre_g", cc), pcol("pre_b", cc), ALU.mult, ALU.add),
                    waits=(ev_last,) if cc == 0 else (), inc="v")
            tbnd = kb.op("vector", lambda e: e.tensor_copy(
                BND[:].unsqueeze(2), XNH[:, :, 1024:1025]), waits=(taff,),
                inc="v")
            tmix = None
            for cc in range(NC):
                xn_c = XNH[:, cc, 1:1025]
                xs_c = XNH[:, cc, 0:1024]
                for dst, mname in ((XK, "mix_k"), (XV, "mix_v"), (XR, "mix_r")):
                    mcol = pcol(mname, cc)
                    ta_ = kb.op("vector", lambda e, xs=xs_c, m=mcol:
                                e.scalar_tensor_tensor(TMB[:], xs, m, xs,
                                                       ALU.mult, ALU.subtract),
                                waits=(tbnd,) if tmix is None else (), inc="v")
                    tmix = kb.op("vector", lambda e, xn=xn_c, m=mcol, dst=dst,
                                 cc=cc: e.scalar_tensor_tensor(
                                     dst[:, cc, :], xn, m, TMB[:],
                                     ALU.mult, ALU.subtract), inc="v")
            t_mix_done = tmix
            if stage_tok is None:
                stage_tok = prev_sc1

            prev_pre = {"k": None, "v": None, "r": None}
            prev_cwo = None
            for c in range(NU):
                slot = c % 8
                BB2 = B2
                BB4 = B4
                BB5 = B5
                wo_extra = [t_mix_done] if c == 0 else (
                    [rw_group_tok[slot]] if slot in rw_group_tok else [])
                t_cwk = kb.dma("sync", WKB[:],
                               wkb_d[c].rearrange("p (c k) -> p c k", c=NC),
                               waits=tuple([precast["wk"][c]] +
                                           ([prev_pre["k"]] if prev_pre["k"]
                                            else [])), sem="dWk")
                t_cwv = kb.dma("sync", WVB[:],
                               wvb_d[c].rearrange("p (c k) -> p c k", c=NC),
                               waits=tuple([precast["wv"][c]] +
                                           ([prev_pre["v"]] if prev_pre["v"]
                                            else [])), sem="dWv")
                t_cwr = kb.dma("sync", WRB[:],
                               wrb_d[c].rearrange("p (c k) -> p c k", c=NC),
                               waits=tuple([precast["wr"][c]] +
                                           ([prev_pre["r"]] if prev_pre["r"]
                                            else [])), sem="dWr")
                t_cwo = kb.dma("sync", WOS[:, slot, :], wob_d[c],
                               waits=tuple([precast["wo"][c]] + wo_extra +
                                           ([prev_cwo] if prev_cwo else [])),
                               sem="dWo")
                prev_cwo = t_cwo

                def mm_pre(ps, wb, xb, waits):
                    tl = None
                    for d in range(NC):
                        for s2 in range(2):
                            tl = kb.op("tensor", lambda e, BB2=BB2, BB4=BB4, BB5=BB5, d=d, s2=s2, ps=ps,
                                       wb=wb, xb=xb: e.matmul(
                                           ps[:, s2 * 512:(s2 + 1) * 512],
                                           wb[:, d, :],
                                           xb[:, d, s2 * 512:(s2 + 1) * 512],
                                           start=(d == 0), stop=(d == NC - 1)),
                                       waits=tuple(waits)
                                       if (d == 0 and s2 == 0) else (),
                                       inc="t" if (d == NC - 1 and s2 == 1)
                                       else None)
                    return tl

                wk_w = [t_cwk, t_mix_done]
                if chain["t_relu"]:
                    wk_w.append(chain["t_relu"])
                t_kpre = mm_pre(PS0, WKB, XK, wk_w)
                wv_w = [t_cwv]
                if chain["t_v"]:
                    wv_w.append(chain["t_v"])
                t_vpre = mm_pre(PS1, WVB, XV, wv_w)
                wr_w = [t_cwr]
                if chain["t_sr"]:
                    wr_w.append(chain["t_sr"])
                if sg_ps2_evac[0] is not None:
                    wr_w.append(sg_ps2_evac[0])
                    sg_ps2_evac[0] = None
                t_rpre = mm_pre(PS2, WRB, XR, wr_w)
                prev_pre = {"k": t_kpre, "v": t_vpre, "r": t_rpre}

                w_relu = [t_kpre]
                t_relu = kb.op("scalar", lambda e, BB2=BB2, BB4=BB4, BB5=BB5: e.activation(
                    LNS[:], PS0[:], AF.Relu, bias=stc(C_60), scale=-1.0),
                    waits=tuple(w_relu), inc="s")
                w_k = []
                if chain["t_scank"]:
                    w_k.append(chain["t_scank"])
                if chain["t_kv"]:
                    w_k.append(chain["t_kv"])
                t_k = kb.op("scalar", lambda e, BB2=BB2, BB4=BB4, BB5=BB5: e.activation(
                    B1[:], LNS[:], AF.Exp, bias=stc(C_60), scale=-1.0),
                    waits=tuple(w_k), inc="s")
                w_v = [t_vpre]
                if chain["t_t1"]:
                    w_v.append(chain["t_t1"])
                t_v = kb.op("scalar", lambda e, BB2=BB2, BB4=BB4, BB5=BB5: e.copy(BB2[:], PS1[:]),
                            waits=tuple(w_v), inc="s")

                ck = ST[:, 64 + c:65 + c]
                ckv = ST[:, 96 + c:97 + c]
                dcl = DRV[:, DV["decay"] + c:DV["decay"] + c + 1]
                fcl = DRV[:, DV["first"] + c:DV["first"] + c + 1]
                w_sk = [t_k, carry_zero]
                if chain["t_rwkv"]:
                    w_sk.append(chain["t_rwkv"])
                t_scank = kb.op("vector", lambda e, BB2=BB2, BB4=BB4, BB5=BB5, d=dcl, ck=ck:
                                e.tensor_tensor_scan(
                                    BB4[:], d.broadcast_to((128, 1024)), B1[:],
                                    ck, ALU.mult, ALU.add),
                                waits=tuple(w_sk), inc="v")
                t_kv = kb.op("gpsimd", lambda e, BB2=BB2: e.tensor_mul(
                    B3[:], B1[:], BB2[:]), waits=(t_k, t_v), inc="g")
                kb.op("vector", lambda e, BB2=BB2, BB4=BB4, BB5=BB5, f=fcl: e.scalar_tensor_tensor(
                    BB2[:, 1:1024], B1[:, 1:1024], f, BB4[:, 0:1023],
                    ALU.mult, ALU.add), waits=(t_scank, t_kv), inc=None)
                kb.op("vector", lambda e, BB2=BB2, BB4=BB4, BB5=BB5, f=fcl, ck=ck: e.scalar_tensor_tensor(
                    BB2[:, 0:1], B1[:, 0:1], f, ck, ALU.mult, ALU.add))
                t_wk = kb.op("vector", lambda e, BB2=BB2, BB4=BB4, BB5=BB5, ck=ck: e.tensor_copy(
                    ck, BB4[:, 1023:1024]), inc="v")
                w_skv = []
                if chain["t_rwkv"]:
                    w_skv.append(chain["t_rwkv"])
                t_scankv = kb.op("vector", lambda e, BB2=BB2, BB4=BB4, BB5=BB5, d=dcl, ckv=ckv:
                                 e.tensor_tensor_scan(
                                     BB5[:], d.broadcast_to((128, 1024)), B3[:],
                                     ckv, ALU.mult, ALU.add),
                                 waits=tuple(w_skv), inc="v")
                kb.op("vector", lambda e, BB2=BB2, BB4=BB4, BB5=BB5, f=fcl: e.scalar_tensor_tensor(
                    BB4[:, 1:1024], B3[:, 1:1024], f, BB5[:, 0:1023],
                    ALU.mult, ALU.add))
                kb.op("vector", lambda e, BB2=BB2, BB4=BB4, BB5=BB5, f=fcl, ckv=ckv: e.scalar_tensor_tensor(
                    BB4[:, 0:1], B3[:, 0:1], f, ckv, ALU.mult, ALU.add))
                t_wkv = kb.op("vector", lambda e, BB2=BB2, BB4=BB4, BB5=BB5, ckv=ckv: e.tensor_copy(
                    ckv, BB5[:, 1023:1024]), inc="v")

                kb.op("scalar", lambda e, BB2=BB2, BB4=BB4, BB5=BB5: e.activation(
                    BB2[:], BB2[:], AF.Ln, bias=stc(C_KEPS)), waits=(t_wk,),
                    inc="s")
                t_rec = kb.op("scalar", lambda e, BB2=BB2, BB4=BB4, BB5=BB5: e.activation(
                    BB2[:], BB2[:], AF.Exp, scale=-1.0), inc="s")
                t_sr = kb.op("scalar", lambda e, BB2=BB2, BB4=BB4, BB5=BB5: e.activation(
                    BB5[:], PS2[:], AF.Sigmoid), waits=(t_rpre, t_wkv), inc="s")

                t_t1 = kb.op("gpsimd", lambda e, BB2=BB2, BB4=BB4:
                             e.tensor_mul(BB4[:], BB4[:], BB2[:]),
                             waits=(t_wkv, t_rec), inc="g")
                w_rw = [t_sr, t_t1]
                if slot in rw_group_tok:
                    w_rw.append(rw_group_tok[slot])
                t_rwkv = kb.op("gpsimd", lambda e, BB4=BB4, BB5=BB5,
                               slot=slot: e.tensor_mul(
                    RW[:, slot, :], BB4[:], BB5[:]), waits=tuple(w_rw),
                    inc="g")
                t_prev_rwkv[slot] = t_rwkv

                hist["t1"] = [t_t1, hist["t1"][0]]
                hist["rwkv"] = [t_rwkv, hist["rwkv"][0]]
                chain.update(t_k=t_k, t_v=t_v, t_sr=t_sr, t_rec=t_rec,
                             t_kv=t_kv, t_wk=t_wk, t_wkv=t_wkv, t_t1=t_t1,
                             t_rwkv=t_rwkv, t_scank=t_scank, t_relu=t_relu)

                if slot == 7:
                    gidx = c // 8
                    adds = []
                    evacs_sg = []
                    sg_last = None
                    for tl8 in range(8):
                        ps = PS3 if tl8 % 2 == 0 else PS2
                        wsg = [t_prev_rwkv[j] for j in range(8)]
                        wsg.append(t_cwo)
                        if tl8 == 1:
                            wsg.append(chain["t_sr"])
                        if tl8 >= 2:
                            wsg.append(evacs_sg[tl8 - 2])
                        tcg = h * 8 + tl8
                        tl_ = None
                        for j in range(8):
                            for os in range(2):
                                tl_ = kb.op("tensor", lambda e, j=j, os=os,
                                            ps=ps, tl8=tl8: e.matmul(
                                        ps[:, os * 512:(os + 1) * 512],
                                        RW[:, j, tl8 * 128:(tl8 + 1) * 128],
                                        WOS[:, j, os * 512:(os + 1) * 512],
                                        start=(j == 0), stop=(j == 7)),
                                    waits=tuple(wsg)
                                    if (j == 0 and os == 0) else (),
                                    inc="t" if (j == 7 and os == 1) else None)
                        sg_last = tl_
                        cv = SGA[:] if tl8 % 2 == 0 else SGB[:]
                        wev = [tl_]
                        if len(adds) >= 2:
                            wev.append(adds[tl8 - 2])
                        elif sg_prev_tail[1]:
                            wev.append(sg_prev_tail[0] if tl8 == 0
                                       else sg_prev_tail[1])
                        tev_ = kb.op("scalar", lambda e, ps=ps, cv=cv: e.copy(
                            cv[:], ps[:]), waits=tuple(wev), inc="s")
                        evacs_sg.append(tev_)
                        ta_ = kb.op("gpsimd", lambda e, cv=cv, tcg=tcg:
                                    e.tensor_add(YT[:, tcg, :], YT[:, tcg, :],
                                                 cv[:]), waits=(tev_,), inc="g")
                        adds.append(ta_)
                    for j in range(8):
                        rw_group_tok[j] = sg_last
                    sg_ps2_evac[0] = evacs_sg[-1]
                    sg_prev_tail = [adds[-2], adds[-1]]

        kb.barrier()

    if PHASES >= 4:
        prev = None
        t_cwr_done = None
        for cc in range(NC):
            td = kb.dma("sync", SC1[:], cwrT[:, cc, :],
                        waits=(prev,) if prev else (), sem="dW")
            t_cwr_done = kb.op("scalar", lambda e, cc=cc: e.copy(
                CWR[:, cc, :], SC1[:]), waits=(td,), inc="s")
            prev = t_cwr_done

        ps_free = [None, None, None]
        sig_tok = None
        xk2_tok = None
        for h in range(2):
            kb.barrier()
            pst_state["tok"] = [kb.tok("s"), kb.tok("s")]
            if h == 0:
                tz = kb.op("vector", lambda e: e.memset(XNH[:, :, 0:1], 0.0),
                           inc="v")
            else:
                tz = kb.op("vector", lambda e: e.tensor_copy(
                    XNH[:, :, 0:1], BND[:].unsqueeze(2)), inc="v")
            ev_last = None
            prev_pe = None
            for tcl in range(NC):
                tc = h * 8 + tcl
                g = tcl % 8
                m_c, r_c, tr = ln_stats(YT[:, tc, :], g)
                wn = [tr, tz]
                if prev_pe:
                    wn.append(prev_pe)
                tn = kb.op("vector", lambda e, tc=tc, m=m_c, r=r_c:
                           e.tensor_scalar(TMB[:], YT[:, tc, :], m, r,
                                           ALU.subtract, ALU.mult),
                           waits=tuple(wn), inc="v")
                prev_pe, ev_last = transpose_tile(
                    TMB[:], XNH[:, :, 1 + tcl * 128:1 + (tcl + 1) * 128], tn)
            taff = None
            for cc in range(NC):
                taff = kb.op("vector", lambda e, cc=cc: e.tensor_scalar(
                    XNH[:, cc, 1:1025], XNH[:, cc, 1:1025],
                    pcol("post_g", cc), pcol("post_b", cc), ALU.mult, ALU.add),
                    waits=(ev_last,) if cc == 0 else (), inc="v")
            tbnd = kb.op("vector", lambda e: e.tensor_copy(
                BND[:].unsqueeze(2), XNH[:, :, 1024:1025]), waits=(taff,),
                inc="v")
            tmix = None
            for cc in range(NC):
                xn_c = XNH[:, cc, 1:1025]
                xs_c = XNH[:, cc, 0:1024]
                for dst, mname in ((XK2[:, cc, h * 1024:(h + 1) * 1024],
                                    "cmix_k"),
                                   (XR2[:, cc, :], "cmix_r")):
                    mcol = pcol(mname, cc)
                    kb.op("vector", lambda e, xs=xs_c, m=mcol:
                          e.scalar_tensor_tensor(TMB[:], xs, m, xs,
                                                 ALU.mult, ALU.subtract),
                          waits=(tbnd,) if tmix is None else ())
                    tmix = kb.op("vector", lambda e, xn=xn_c, m=mcol, dst=dst:
                                 e.scalar_tensor_tensor(dst, xn, m, TMB[:],
                                                        ALU.mult,
                                                        ALU.subtract), inc="v")
            xk2_tok = tmix
            for tcl in range(NC):
                tc = h * 8 + tcl
                ps = PSF[tcl % 2]
                w = [tmix, t_cwr_done]
                if ps_free[tcl % 2]:
                    w.append(ps_free[tcl % 2])
                tl = None
                for d in range(NC):
                    for os in range(2):
                        tl = kb.op("tensor", lambda e, d=d, os=os, ps=ps,
                                   tcl=tcl: e.matmul(
                                       ps[:, os * 512:(os + 1) * 512],
                                       XR2[:, d, tcl * 128:(tcl + 1) * 128],
                                       CWR[:, d, os * 512:(os + 1) * 512],
                                       start=(d == 0), stop=(d == NC - 1)),
                                   waits=tuple(w) if (d == 0 and os == 0)
                                   else (),
                                   inc="t" if (d == NC - 1 and os == 1)
                                   else None)
                wsg = [tl]
                if sig_tok:
                    wsg.append(sig_tok)
                tsg = kb.op("scalar", lambda e, ps=ps: e.activation(
                    LNS[:], ps[:], AF.Sigmoid), waits=tuple(wsg), inc="s")
                ps_free[tcl % 2] = tsg
                sig_tok = kb.op("gpsimd", lambda e, tc=tc: e.tensor_add(
                    YT[:, tc, :], YT[:, tc, :], LNS[:]), waits=(tsg,), inc="g")

        kb.barrier()
        stage_tok2 = None
        silu_tok = {}
        g_group_tok = {}
        prev_adds_tail = [None, None, None]
        prev_k2 = None
        prev_cv4 = None
        sg_ps2_evac4 = [None]
        for c in range(NU):
            slot = c % 8
            wck = [precast["cwk"][c], xk2_tok]
            if prev_k2:
                wck.append(prev_k2)
            t_ck = kb.dma("sync", WKB[:],
                          cwkb_d[c].rearrange("p (c k) -> p c k", c=NC),
                          waits=tuple(wck), sem="dWk")
            wcv = [precast["cwv"][c]]
            if slot in g_group_tok:
                wcv.append(g_group_tok[slot])
            else:
                wcv.append(xk2_tok)
            if prev_cv4:
                wcv.append(prev_cv4)
            t_cv = kb.dma("sync", WOS[:, slot, :], cwvb_d[c],
                          waits=tuple(wcv), sem="dWo")
            prev_cv4 = t_cv

            wmm = [t_ck, xk2_tok]
            if c > 0:
                wmm.append(silu_tok[c - 1])
            tl = None
            for d in range(NC):
                for s4 in range(4):
                    ps = PS0 if s4 < 2 else PS1
                    col = (s4 % 2) * 512
                    tl = kb.op("tensor", lambda e, d=d, s4=s4, ps=ps, col=col:
                               e.matmul(ps[:, col:col + 512], WKB[:, d, :],
                                        XK2[:, d, s4 * 512:(s4 + 1) * 512],
                                        start=(d == 0), stop=(d == NC - 1)),
                               waits=tuple(wmm) if (d == 0 and s4 == 0) else (),
                               inc="t" if (d == NC - 1 and s4 == 3) else None)
            wsl = [tl]
            if slot in g_group_tok:
                wsl.append(g_group_tok[slot])
            prev_k2 = tl
            ts1_ = kb.op("scalar", lambda e, slot=slot: e.activation(
                GSL[:, slot, 0:1024], PS0[:], AF.Silu), waits=tuple(wsl),
                inc="s")
            silu_tok[c] = kb.op("scalar", lambda e, slot=slot: e.activation(
                GSL[:, slot, 1024:2048], PS1[:], AF.Silu), inc="s")

            if slot == 7:
                adds = []
                evacs_sg = []
                sg_last = None
                for tc in range(NT):
                    ps = PS3 if tc % 2 == 0 else PS2
                    wsg = [silu_tok[c]]
                    wsg.append(t_cv)
                    if tc >= 2:
                        wsg.append(evacs_sg[tc - 2])
                    tl_ = None
                    for j in range(8):
                        for os in range(2):
                            tl_ = kb.op("tensor", lambda e, j=j, os=os, ps=ps,
                                        tc=tc: e.matmul(
                                    ps[:, os * 512:(os + 1) * 512],
                                    GSL[:, j, tc * 128:(tc + 1) * 128],
                                    WOS[:, j, os * 512:(os + 1) * 512],
                                    start=(j == 0), stop=(j == 7)),
                                waits=tuple(wsg) if (j == 0 and os == 0)
                                else (),
                                inc="t" if (j == 7 and os == 1) else None)
                    sg_last = tl_
                    cv = SGA[:] if tc % 2 == 0 else SGB[:]
                    wev = [tl_]
                    if len(adds) >= 2:
                        wev.append(adds[tc - 2])
                    elif prev_adds_tail[2]:
                        wev.append(prev_adds_tail[1] if tc == 0
                                   else prev_adds_tail[2])
                    tev_ = kb.op("scalar", lambda e, ps=ps, cv=cv: e.copy(
                        cv, ps[:]), waits=tuple(wev), inc="s")
                    evacs_sg.append(tev_)
                    ta_ = kb.op("gpsimd", lambda e, cv=cv, tc=tc: e.tensor_add(
                        YT[:, tc, :], YT[:, tc, :], cv), waits=(tev_,),
                        inc="g")
                    adds.append(ta_)
                for j in range(8):
                    g_group_tok[j] = sg_last
                prev_adds_tail = [adds[-3], adds[-2], adds[-1]]
                sg_ps2_evac4[0] = evacs_sg[-1]

        kb.barrier()
        adds_fin = []
        EBS4 = [SC1, SC2, EB3, EB4]
        for tc in range(NT):
            wdma = [adds_fin[tc - 2]] if tc >= 2 else ()
            rtok = kb.dma("sync", XST[:, tc % 2, :], xpark[:, tc, :],
                          waits=tuple(wdma), sem="dT%d" % (tc % 2))
            g = tc % 8
            eb = EBS4[tc % 4]
            m_c, r_c, tr = ln_stats(YT[:, tc, :], g)
            wn = [tr]
            if tc >= 4:
                wn.append(adds_fin[tc - 4])
            tn = kb.op("vector", lambda e, tc=tc, m=m_c, r=r_c, eb=eb:
                       e.tensor_scalar(eb[:], YT[:, tc, :], m, r,
                                       ALU.subtract, ALU.mult),
                       waits=tuple(wn), inc="v")
            adds_fin.append(kb.op("gpsimd", lambda e, tc=tc, eb=eb:
                                  e.tensor_add(XT[:, tc, :], XST[:, tc % 2, :],
                                               eb[:]), waits=(tn, rtok),
                                  inc="g"))
        t_p4_sgevac = sg_ps2_evac4[0]
        t_p4_silu_last = silu_tok[NU - 1]

    if PHASES >= 5:
        tb = kb.dma("sync", BCA[:], bias_v.unsqueeze(0).broadcast_to((128, D)),
                    waits=(kb.tok("dB"), kb.tok("g")), sem="dB")
        prev = None
        t_wff = None
        t_pe_all = kb.tok("t")
        for cc in range(NC):
            td = kb.dma("sync", ISTG[:], wffT[:, cc, :],
                        waits=(prev,) if prev else (t_pe_all,), sem="dW")
            t_wff = kb.op("vector", lambda e, cc=cc: e.tensor_copy(
                WFF[:, cc, :], ISTG[:]), waits=(td,), inc="v")
            prev = t_wff

        pst_state["tok"] = [t_p4_sgevac, t_p4_sgevac]
        tq_tile = {}
        qrot5 = [adds_fin[12], adds_fin[13], adds_fin[14], adds_fin[15]]
        prev_round = None
        for tc in range(NT):
            g = tc % 8
            wv_t = (adds_fin[tc],)
            m_c, r_c, tr = ln_stats(
                XT[:, tc, :], g, waits_v=wv_t,
                waits_s=wv_t + ((prev_round,) if prev_round else ()),
                base=184)
            qb = EBS[tc % 4]
            wn = [tr]
            if qrot5[tc % 4]:
                wn.append(qrot5[tc % 4])
            tn = kb.op("vector", lambda e, tc=tc, m=m_c, r=r_c, qb=qb:
                       e.tensor_scalar(qb[:], XT[:, tc, :], m, r,
                                       ALU.subtract, ALU.mult),
                       waits=tuple(wn), inc="v")
            wsq = [tn]
            if prev_round:
                wsq.append(prev_round)
            tsq = kb.op("scalar", lambda e, qb=qb: e.activation(
                LNS[:], qb[:], AF.Square, accum_out=stc(130)), waits=tuple(wsq),
                inc="s")
            tr2a = kb.op("scalar", lambda e: e.activation(
                stc(131), stc(130), AF.Ln, bias=stc(C_EPSRMS),
                scale=1.0 / D), waits=(tsq,), inc="s")
            tr2 = kb.op("scalar", lambda e: e.activation(
                stc(131), stc(131), AF.Exp, scale=-0.5), inc="s")
            tam = kb.op("vector", lambda e, qb=qb: e.reduce_max(
                stc(132), qb[:], axis=AX.X, apply_absolute_value=True),
                waits=(tn,), inc="v")
            tax = kb.op("vector", lambda e: e.tensor_mul(
                stc(132), stc(132), stc(131)), waits=(tr2,), inc="v")
            tcl_ = kb.op("vector", lambda e: e.tensor_scalar_max(
                stc(132), stc(132), EPS_ACT), inc="v")
            kb.op("scalar", lambda e: e.activation(
                stc(133), stc(132), AF.Ln, scale=1.0 / 127.0), waits=(tcl_,))
            tlsr = kb.op("scalar", lambda e: e.activation(
                stc(133), stc(133), AF.Exp, scale=-1.0), inc="s")
            tr2s = kb.op("vector", lambda e: e.tensor_mul(
                stc(133), stc(133), stc(131)), waits=(tlsr,), inc="v")
            kb.op("vector", lambda e, tc=tc: e.tensor_scalar_mul(
                stc(24 + tc), stc(132), WSCALE / 127.0), inc="v")
            wq1 = [tr2s]
            if prev_round:
                wq1.append(prev_round)
            tq1 = kb.op("scalar", lambda e, qb=qb: e.activation(
                LNS[:], qb[:], AF.Copy, scale=stc(133)), waits=tuple(wq1),
                inc="s")
            qrot5[tc % 4] = tq1
            tq2 = kb.op("vector", lambda e, tc=tc: e.tensor_scalar(
                QTM[:, tc, :], LNS[:], RND, -RND, ALU.add, ALU.add),
                waits=(tq1,), inc="v")
            prev_round = tq2
            _, tq_tile[tc] = transpose_tile(
                QTM[:, tc, :], QFM[:, :, tc * 128:(tc + 1) * 128], tq2)

        ps_free = [t_p4_silu_last, t_p4_silu_last, t_p4_sgevac]
        add_toks = list(qrot5)
        for tc in range(NT):
            ps = PSF[tc % 3]
            w = [tq_tile[tc], t_wff]
            if ps_free[tc % 3]:
                w.append(ps_free[tc % 3])
            tl = None
            for d in range(NC):
                for os in range(2):
                    tl = kb.op("tensor", lambda e, d=d, os=os, ps=ps, tc=tc:
                               e.matmul(ps[:, os * 512:(os + 1) * 512],
                                        QFM[:, d, tc * 128:(tc + 1) * 128],
                                        WFF[:, d, os * 512:(os + 1) * 512],
                                        start=(d == 0), stop=(d == NC - 1)),
                               waits=tuple(w) if (d == 0 and os == 0) else (),
                               inc="t" if (d == NC - 1 and os == 1) else None)
            wn = [tl, tb]
            if add_toks[tc % 2]:
                wn.append(add_toks[tc % 2])
            eb = EBS[tc % 4]
            ta = kb.op("scalar", lambda e, ps=ps, tc=tc, eb=eb: e.activation(
                eb[:], ps[:], AF.Copy, scale=stc(24 + tc)), waits=tuple(wn),
                inc="s")
            ps_free[tc % 3] = ta
            tb2_ = kb.op("gpsimd", lambda e, eb=eb: e.tensor_add(
                eb[:], eb[:], BCA[:]), waits=(ta,), inc="g")
            g = tc % 8
            m_c, r_c, tr = ln_stats(eb[:], g, waits_v=(tb2_,), waits_s=(tb2_,))
            tn = kb.op("vector", lambda e, m=m_c, r=r_c, eb=eb: e.tensor_scalar(
                eb[:], eb[:], m, r, ALU.subtract, ALU.mult), waits=(tr,),
                inc="v")
            add_toks[tc % 4] = kb.op("gpsimd", lambda e, tc=tc, eb=eb:
                                     e.tensor_add(XT[:, tc, :], XT[:, tc, :],
                                                  eb[:]), waits=(tn,), inc="g")
            kb.dma("sync", out_d[:, tc, :], XT[:, tc, :],
                   waits=(add_toks[tc % 4],), sem="o")
        kb.wait("sync", kb.tok("o"))
        kb.wait("gpsimd", kb.tok("o"))
    else:
        src = YT if PHASES == 3 else XT
        for tc in range(NT):
            kb.dma("scalar", out_d[:, tc, :], src[:, tc, :], sem="o")
        kb.wait("sync", kb.tok("o"))
        kb.wait("gpsimd", kb.tok("o"))

    kb.emit(nc)
    return nc


def kernel(**inputs):
    f32 = np.float32
    g = {k: np.asarray(v) for k, v in inputs.items()}

    x = np.ascontiguousarray(g["L0_attn_x"], dtype=f32)
    w_inT = _wpack(np.ascontiguousarray(g["L0_attn_W_in"].T))
    w_outT = _wpack(np.ascontiguousarray(g["L0_attn_W_out"].T))
    ffn0_rhs = _wpack(g["L0_ffn_rhs"])
    wk_u = _upack(g["L1_attn_tm_Wk"])
    wv_u = _upack(g["L1_attn_tm_Wv"])
    wr_u = _upack(g["L1_attn_tm_Wr"])
    wo_u = np.ascontiguousarray(np.asarray(g["L1_attn_tm_Wo"], dtype=f32)
                                .reshape(NU, 128, D))
    cwk_u = _upack(g["L1_attn_cm_Wk"])
    cwv_u = np.ascontiguousarray(np.asarray(g["L1_attn_cm_Wv"], dtype=f32)
                                 .reshape(NU, 128, D))
    cwrT = _wpack(g["L1_attn_cm_Wr"])
    wffT = _wpack(np.ascontiguousarray(g["L1_ffn_w_int8"].T).astype(np.int32))

    prm = np.zeros((128, 544), f32)
    col = 0
    for name, arr, w in [
        ("A_log", g["L0_attn_A_log"], 128),
        ("B_ssm", g["L0_attn_B_ssm"], 128),
        ("C_ssm", g["L0_attn_C_ssm"], 128),
        ("dt_log", g["L0_attn_dt_log"], 8),
        ("D_ssm", g["L0_attn_D_ssm"], 8),
        ("decay", g["L1_attn_tm_decay"], 32),
        ("first", g["L1_attn_tm_first"], 32),
        ("mix_k", g["L1_attn_tm_mix_k"], 8),
        ("mix_v", g["L1_attn_tm_mix_v"], 8),
        ("mix_r", g["L1_attn_tm_mix_r"], 8),
        ("cmix_k", g["L1_attn_cm_mix_k"], 8),
        ("cmix_r", g["L1_attn_cm_mix_r"], 8),
        ("pre_g", g["L1_attn_pre_g"], 8),
        ("pre_b", g["L1_attn_pre_b"], 8),
        ("post_g", g["L1_attn_post_g"], 8),
        ("post_b", g["L1_attn_post_b"], 8),
    ]:
        arr = np.asarray(arr, dtype=f32)
        if name in ("decay", "first"):
            pm = np.ascontiguousarray(arr.reshape(32, 128).T)
        else:
            pm = _pm(arr)
        prm[:, col:col + w] = pm
        col += w

    base = dict(
        w_inT=w_inT.astype(f32), w_outT=w_outT.astype(f32),
        ffn0_rhs=ffn0_rhs.astype(f32),
        wk_u=wk_u.astype(f32), wv_u=wv_u.astype(f32), wr_u=wr_u.astype(f32),
        wo_u=wo_u, cwk_u=cwk_u.astype(f32), cwv_u=cwv_u, cwrT=cwrT.astype(f32),
        wffT=wffT, prm=prm,
        ln0g_v=np.asarray(g["L1_attn_ln0_g"], f32),
        ln0b_v=np.asarray(g["L1_attn_ln0_b"], f32),
        bias_v=np.asarray(g["L1_ffn_bias"], f32),
        ident_b=np.eye(128).astype(ml_dtypes.bfloat16),
        ident_f=np.eye(128, dtype=f32),
    )

    nc = build_program(np.asarray(g["L1_ffn_weight_scale"], f32))
    in_maps = []
    for core in range(8):
        xc = np.ascontiguousarray(
            x[core].reshape(NT, 128, D).transpose(1, 0, 2))
        in_maps.append(dict(base, x=xc))
    res = run_bass_kernel_spmd(nc, in_maps, core_ids=list(range(8)))
    outs = []
    for core in range(8):
        o = res.results[core]["out"]
        outs.append(o.transpose(1, 0, 2).reshape(T, D))
    return np.stack(outs, axis=0).astype(np.float32)


if __name__ == "__main__":
    z = {k: np.zeros(s, dtype=d) for k, s, d in [("t", (1,), np.float32)]}
    print("kernel module loaded")
